# version 13
# baseline (speedup 1.0000x reference)
import contextlib
import numpy as np

import concourse.bass as bass
import concourse.mybir as mybir
from concourse.alu_op_type import AluOpType as OP
from concourse.bass_utils import run_bass_kernel_spmd

F32 = mybir.dt.float32
BF16 = mybir.dt.bfloat16
AF = mybir.ActivationFunctionType
AX = mybir.AxisListType

L = 4096
D = 1024
DI = 2048
NST = 16
RK = 64
NE = 8
FF = 2048
TC = 256
TF = 512
NCH = L // TC
NFC = L // TF
EPS = 1e-5


class Sem:
    def __init__(self, es, nc, name):
        self.h = es.enter_context(nc.semaphore(name))
        self.n = 0

    def inc(self, ins, k=1):
        self.n += k
        ins.then_inc(self.h, k)
        return (self, self.n)

    @property
    def ev(self):
        return (self, self.n)


class B:

    def __init__(self, nc, es):
        self.nc = nc
        self.es = es
        self._waited = {}

    def sem(self, name):
        return Sem(self.es, self.nc, name)

    def sb(self, name, shape, dt):
        return self.es.enter_context(self.nc.sbuf_tensor(name, shape, dt))

    def ps(self, name, shape, dt):
        return self.es.enter_context(self.nc.psum_tensor(name, shape, dt))

    def W(self, eng, ev):
        if ev is None:
            return
        if isinstance(ev, list):
            for e in ev:
                self.W(eng, e)
            return
        sem, n = ev
        if n <= 0:
            return
        assert n <= sem.n, f"wait target {n} > emitted {sem.n}: deadlock"
        key = (id(eng), id(sem))
        if self._waited.get(key, 0) >= n:
            return
        eng.wait_ge(sem.h, n)
        self._waited[key] = n


def build_nc(dbg=False):
    nc = bass.Bass("TRN2", target_bir_lowering=False,
                   detect_race_conditions=False)
    es = contextlib.ExitStack()
    bb = B(nc, es)

    dr = lambda n, s, t, k: nc.dram_tensor(n, s, t, kind=k).ap()
    x_h = dr("x", [L, D], F32, "ExternalInput")
    out_h = dr("out", [L, D], F32, "ExternalOutput")
    w_in_h = dr("w_in", [D, 2 * DI], F32, "ExternalInput")
    w_out_h = dr("w_out", [DI, D], F32, "ExternalInput")
    w_xp_h = dr("w_xp", [DI, 96], F32, "ExternalInput")
    w_dt_h = dr("w_dt", [RK, DI], F32, "ExternalInput")
    w1_h = dr("w1", [NE, D, FF], F32, "ExternalInput")
    w2_h = dr("w2", [NE, FF, D], F32, "ExternalInput")
    wg_h = dr("wg", [D, NE], F32, "ExternalInput")
    lng_h = dr("lng", [1, D], F32, "ExternalInput")
    lnb_h = dr("lnb", [1, D], F32, "ExternalInput")
    gam_h = dr("gam", [1, D], F32, "ExternalInput")
    bet_h = dr("bet", [1, D], F32, "ExternalInput")
    alog_h = dr("alog", [128, 16 * NST], F32, "ExternalInput")
    cw_h = dr("cw", [128, 16 * 4], F32, "ExternalInput")
    cb_h = dr("cb", [128, 16], F32, "ExternalInput")
    bdt_h = dr("bdt", [128, 16], F32, "ExternalInput")
    dpar_h = dr("dpar", [128, 16], F32, "ExternalInput")
    b1_h = dr("b1", [128, 128], F32, "ExternalInput")
    b2_h = dr("b2", [NE, D], F32, "ExternalInput")
    bg_h = dr("bg", [NE, 1], F32, "ExternalInput")

    sk = "ExternalOutput" if dbg else "Internal"
    dbg_outs = {}
    def dbg_decl(name, shape, dt=F32):
        if dbg:
            dbg_outs[name] = nc.dram_tensor("dbg_" + name, shape, dt, kind="ExternalOutput").ap()
    winb_s = dr("winb_s", [32, 128, 1024], BF16, "Internal")
    woutb_s = dr("woutb_s", [16, 128, 1024], BF16, "Internal")
    w1b_s = dr("w1b_s", [NE * 16, 128, 1024], BF16, "Internal")
    w2b_s = dr("w2b_s", [NE * 16, 128, 1024], BF16, "Internal")
    x1_s = dr("x1_s", [L, D], F32, sk)
    xn1_s = dr("xn1_s", [L, D], F32, sk)
    xn1T_s = dr("xn1T_s", [D, L], BF16, "Internal")
    g_s = dr("g_s", [NE, L], BF16, sk)
    gt_s = dr("gt_s", [L, NE], F32, "Internal")

    dbg_decl("h0T0", [128, TC], BF16); dbg_decl("RT0", [128, TC])
    dbg_decl("xin0", [128, TC + 3], BF16); dbg_decl("zsil0", [128, TC], BF16)
    dbg_decl("dlt0", [128, TC]); dbg_decl("du0", [128, TC], BF16)
    dbg_decl("a0", [128, TC], BF16); dbg_decl("Bbc0", [128, TC], BF16)
    dbg_decl("Cbc0", [128, TC], BF16); dbg_decl("s0", [128, TC], BF16)
    dbg_decl("yacc0", [128, TC]); dbg_decl("yg0", [128, TC], BF16)
    dbg_decl("A0", [128, 16 * NST]); dbg_decl("dtb0", [RK, TC], BF16)
    dbg_decl("xb0", [128, D]); dbg_decl("sc0", [128, 16]); dbg_decl("yb0", [128, D])
    dbg_decl("xpb", [128, 16 * 96], BF16); dbg_decl("xcall", [128, 16 * TC], BF16)
    dbg_decl("mix0", [128, D]); dbg_decl("gh0", [128, TF], BF16); dbg_decl("gbcf0", [128, TF], BF16)
    with es:
        sb = bb.sb
        ones_bf = sb("ones_bf", [1, 128], BF16)
        ones_f = sb("ones_f", [1, 128], F32)
        idf = sb("idf", [128, 128], F32)
        g_bc = sb("g_bc", [128, D], BF16)
        bb_bc = sb("bb_bc", [128, D], BF16)
        gam_bc = sb("gam_bc", [128, D], BF16)
        bet_bc = sb("bet_bc", [128, D], BF16)
        xprojb = sb("xprojb", [128, 16 * 96], BF16)
        wdtb = sb("wdtb", [RK, DI], BF16)
        wgb = sb("wgb", [128, NE * 8], BF16)
        b2b = sb("b2b", [NE, D], BF16)
        bg_c = sb("bg_c", [NE, 1], F32)
        A_sb = sb("A_sb", [128, 16 * NST], F32)
        cw_sb = sb("cw_sb", [128, 64], F32)
        cb_sb = sb("cb_sb", [128, 16], F32)
        bdt_sb = sb("bdt_sb", [128, 16], F32)
        dp_sb = sb("dp_sb", [128, 16], F32)
        b1_sb = sb("b1_sb", [128, 128], F32)
        row1 = sb("row1", [1, D], F32)
        s_state = sb("s_state", [128, 16 * NST], F32)
        cvals = [EPS, EPS * EPS, 1.0 + EPS, 1.0, -1.0, -0.5, 0.0]
        cst = sb("cst", [128, len(cvals)], F32)
        NRING = 7
        ring = [sb(f"ring{r}", [128, 1024], BF16) for r in range(NRING)]
        xbuf = [sb(f"xbuf{t}", [128, D], F32) for t in range(2)]
        ybuf = [sb(f"ybuf{t}", [128, D], F32) for t in range(2)]
        scol = sb("scol", [128, 32], F32)
        RT = [sb(f"RT{m}", [128, TC], F32) for m in range(8)]
        h0T = [sb(f"h0T{m}", [128, TC], BF16) for m in range(8)]
        xin = [sb(f"xin{i}", [128, TC + 4], BF16) for i in range(16)]
        cacc = [sb(f"cacc{r}", [128, TC], F32) for r in range(2)]
        zsil = [sb(f"zsil{i}", [128, TC], BF16) for i in range(16)]
        dt_b = sb("dt_b", [RK, TC], BF16)
        BC_sb = sb("BC_sb", [128, TC], BF16)
        BCrow = sb("BCrow", [1, 16 * TC], BF16)
        B_bc = [sb(f"Bbc{n}", [128, TC], BF16) for n in range(NST)]
        C_bc = [sb(f"Cbc{n}", [128, TC], BF16) for n in range(NST)]
        dlt = [sb(f"dlt{r}", [128, TC], F32) for r in range(2)]
        du = [sb(f"du{r}", [128, TC], BF16) for r in range(2)]
        sp_t = [sb(f"spt{r}", [128, TC], F32) for r in range(2)]
        a_t = [sb(f"a{r}", [128, TC], BF16) for r in range(4)]
        b_t = [sb(f"bt{r}", [128, TC], BF16) for r in range(2)]
        s_all = sb("s_all", [128, NST * TC], BF16)
        prod = sb("prod", [128, TC], BF16)
        yacc = sb("yacc", [128, TC], F32)
        xn1T = [sb(f"xn1T{m}", [128, TC], BF16) for m in range(8)]
        lg_sb = sb("lg_sb", [NE, TC], F32)
        lgt = [sb(f"lgt{t}", [128, NE], F32) for t in range(2)]
        gcol = sb("gcol", [128, 32], F32)
        msk = [sb(f"msk{t}", [128, NE], F32) for t in range(2)]
        eq1 = [sb(f"eq1{t}", [128, NE], F32) for t in range(2)]
        gts = [sb(f"gts{t}", [128, NE], F32) for t in range(2)]
        g_sb = sb("g_sb", [NE, TC], BF16)
        junk = sb("junk", [128, D], BF16)
        xn1Tf = [sb(f"xn1Tf{kk}", [128, TF], BF16) for kk in range(8)]
        g_f = sb("g_f", [NE, TF], BF16)
        gh = [sb(f"gh{fk}", [128, TF], BF16) for fk in range(8)]
        gtf = sb("gtf", [128, 32], F32)
        mixtmp = [sb(f"mxt{r}", [128, 512], BF16) for r in range(4)]
        w2c = [sb(f"w2c{s}", [128, 1024], BF16) for s in range(8)]
        NTB = TF // 128
        mixed = [sb(f"mix{tb}", [128, D], F32) for tb in range(NTB)]
        xn1pc = sb("xn1pc", [128, D], F32)
        x1pc = sb("x1pc", [128, D], F32)
        t_a = [sb(f"ta{r}", [128, D], F32) for r in range(2)]
        fcol = sb("fcol", [128, 8], F32)

        PB = [bb.ps(f"PB{i}", [128, 512], F32) for i in range(8)]

        T = bb.sem("T")
        V = bb.sem("V")
        S = bb.sem("S")
        G = bb.sem("G")

        sy = nc.sync
        te, ve, se, ge = nc.tensor, nc.vector, nc.scalar, nc.gpsimd

        def vf(ins):
            ev = V.inc(ins)
            bb.W(ve, ev)
            return ev

        def sf(ins):
            ev = S.inc(ins)
            bb.W(se, ev)
            return ev

        def dbg_dump(name, ap, wait):
            if not dbg or name not in dbg_outs:
                return
            bb.W(sy, wait)
            ins = sy.dma_start(out=dbg_outs[name], in_=ap)
            DS("dbg").inc(ins, 16)

        def dma(q, dsem, out, in_, wait=None):
            bb.W(q, wait)
            ins = q.dma_start(out=out, in_=in_)
            return dsem.inc(ins, 16)

        dsm = {}
        def DS(name):
            if name not in dsm:
                dsm[name] = bb.sem("d_" + name)
            return dsm[name]

        ring_free = [None] * NRING
        ring_ptr = [0]

        def ring_next():
            r = ring_ptr[0] % NRING
            ring_ptr[0] += 1
            return r

        def ring_load(src_ap):
            r = ring_next()
            ev = dma(sy, DS(f"ring{r}"), ring[r][:, :], src_ap, wait=ring_free[r])
            return r, ev

        tr_free = {0: None, 1: None}
        tr_ptr = [0]

        def next_tr():
            i = tr_ptr[0] % 2
            tr_ptr[0] += 1
            return i

        aux_free = {2: None, 3: None}
        aux_ptr = [0]

        def next_aux():
            i = 2 + (aux_ptr[0] % 2)
            aux_ptr[0] += 1
            return i

        for ci, cv in enumerate(cvals):
            G.inc(ge.memset(cst[:, ci:ci + 1], cv))
            nc.const_aps.register(F32, cv, cst[:, ci:ci + 1])
        G.inc(ge.memset(ones_bf[:, :], 1.0))
        G.inc(ge.memset(ones_f[:, :], 1.0))
        G.inc(ge.memset(s_state[:, :], 0.0))
        evm = G.inc(ge.memset(idf[:, :], 0.0))
        bb.W(ge, evm)
        ins = nc.gpsimd.affine_select(out=idf[:, :], in_=idf[:, :],
                                      compare_op=OP.not_equal, fill=1.0, base=0,
                                      pattern=[[-1, 128]], channel_multiplier=1)
        G.inc(ins)
        for i in range(16):
            G.inc(ge.memset(xin[i][:, 0:3], 0.0))
        ev_g0 = (G, G.n)

        ev_small = None
        for dst, src in [(A_sb[:, :], alog_h[:, :]), (cw_sb[:, :], cw_h[:, :]),
                         (cb_sb[:, :], cb_h[:, :]), (bdt_sb[:, :], bdt_h[:, :]),
                         (dp_sb[:, :], dpar_h[:, :]), (b1_sb[:, :], b1_h[:, :]),
                         (bg_c[:, :], bg_h[:, :]),
                         ]:
            ev_small = dma(sy, DS("small"), dst, src)
        bb.W(se, ev_g0)
        bb.W(ve, ev_g0)
        bb.W(se, ev_small)
        ins = se.activation(A_sb[:, :], A_sb[:, :], AF.Exp)
        ev_sA = S.inc(ins)
        bb.W(ve, ev_sA)
        ins = ve.tensor_scalar(A_sb[:, :], A_sb[:, :], -1.0, None, op0=OP.mult)
        ev_vA = V.inc(ins)

        row1_free = None
        for ri, (dst, srcrow) in enumerate([(g_bc, lng_h), (bb_bc, lnb_h),
                                            (gam_bc, gam_h), (bet_bc, bet_h)]):
            ev_row = dma(sy, DS("row1"), row1[0:1, :], srcrow[:, :], wait=row1_free)
            for h in range(2):
                pb = next_aux()
                bb.W(te, ev_g0)
                bb.W(te, aux_free[pb])
                bb.W(te, ev_row)
                ins = te.matmul(PB[pb][:, 0:512], ones_f[0:1, :],
                                row1[0:1, h * 512:(h + 1) * 512],
                                start=True, stop=True)
                ev_t = T.inc(ins)
                bb.W(se, ev_t)
                ins = se.activation(dst[:, h * 512:(h + 1) * 512], PB[pb][:, 0:512], AF.Copy)
                aux_free[pb] = S.inc(ins)
            row1_free = (T, T.n)
        ev_bcast = (S, S.n)

        ev_cast = ev_vA
        for kk in range(16):
            evl = dma(sy, DS("stg0"), t_a[0][:, 0:96], w_xp_h[kk * 128:(kk + 1) * 128, :],
                      wait=ev_cast)
            bb.W(ve, evl)
            ins = ve.tensor_copy(xprojb[:, kk * 96:(kk + 1) * 96], t_a[0][:, 0:96])
            ev_cast = V.inc(ins)
        evl = dma(sy, DS("stg0"), t_a[0][0:RK, 0:D], w_dt_h[:, 0:D], wait=ev_cast)
        evl2 = dma(sy, DS("stg1"), t_a[1][0:RK, 0:D], w_dt_h[:, D:DI])
        bb.W(ve, evl)
        ve.tensor_copy(wdtb[:, 0:D], t_a[0][0:RK, 0:D])
        bb.W(ve, evl2)
        ins = ve.tensor_copy(wdtb[:, D:DI], t_a[1][0:RK, 0:D])
        ev_cast = V.inc(ins)
        for kk in range(8):
            evl = dma(sy, DS("stg0"), t_a[0][:, 0:8], wg_h[kk * 128:(kk + 1) * 128, :], wait=ev_cast)
            bb.W(ve, evl)
            ins = ve.tensor_copy(wgb[:, kk * 8:(kk + 1) * 8], t_a[0][:, 0:8])
            ev_cast = V.inc(ins)
        evl = dma(sy, DS("stg0"), t_a[0][0:NE, 0:D], b2_h[:, :], wait=ev_cast)
        bb.W(ve, evl)
        ev_cast = V.inc(ve.tensor_copy(b2b[:, :], t_a[0][0:NE, 0:D]))

        stage_free = [ev_cast, ev_cast]
        q_tgl = [0]

        def prep_slab(src_f32_ap, out_maker):
            q = q_tgl[0]
            q_tgl[0] ^= 1
            evl = dma(nc.gpsimd, DS(f"pstg{q}"), t_a[q][:, :], src_f32_ap,
                      wait=stage_free[q])
            r = ring_next()
            bb.W(ve, ring_free[r])
            bb.W(ve, evl)
            ins = ve.tensor_copy(ring[r][:, :], t_a[q][:, :])
            evc = V.inc(ins)
            stage_free[q] = evc
            out_ap, in_ap = out_maker(ring[r])
            evo = dma(nc.gpsimd, DS(f"pring{r}"), out_ap, in_ap, wait=evc)
            ring_free[r] = evo
            return evo

        for kk in range(8):
            for q in range(4):
                def mk(slot, kk=kk, q=q):
                    o = winb_s[q * 8:(q + 1) * 8, :, kk * 128:(kk + 1) * 128]
                    o = o.rearrange("m p c -> p m c")
                    i = slot[:, :].rearrange("p (m c) -> p m c", m=8)
                    return o, i
                prep_slab(w_in_h[kk * 128:(kk + 1) * 128, q * 1024:(q + 1) * 1024], mk)
        for i in range(16):
            prep_slab(w_out_h[i * 128:(i + 1) * 128, :],
                      lambda slot, i=i: (woutb_s[i, :, :], slot[:, :]))
        ev_prep_all = [(dsm[f"pring{r}"], dsm[f"pring{r}"].n) for r in range(NRING)]

        piece_ev = {}
        slab_queue = []
        for e in range(NE):
            for kk in range(8):
                for h in range(2):
                    def mk(slot, e=e, kk=kk, h=h):
                        o = w1b_s[e * 16 + h * 8: e * 16 + (h + 1) * 8, :,
                                  kk * 128:(kk + 1) * 128]
                        o = o.rearrange("m p c -> p m c")
                        i = slot[:, :].rearrange("p (m c) -> p m c", m=8)
                        return o, i
                    keys = [("w1", e, h * 8 + f) for f in range(8)]
                    slab_queue.append((w1_h[e, kk * 128:(kk + 1) * 128,
                                            h * 1024:(h + 1) * 1024], mk, keys))
            for fk in range(16):
                slab_queue.append((w2_h[e, fk * 128:(fk + 1) * 128, :],
                                   lambda slot, e=e, fk=fk: (w2b_s[e * 16 + fk, :, :],
                                                             slot[:, :]),
                                   [("w2", e, fk)]))
        slab_pos = [0]

        def emit_slabs(nmax):
            n = 0
            while slab_pos[0] < len(slab_queue) and n < nmax:
                srcap, mkf, keys = slab_queue[slab_pos[0]]
                evo = prep_slab(srcap, mkf)
                for kkey in keys:
                    ev_old = piece_ev.get(kkey)
                    piece_ev[kkey] = [evo] if ev_old is None else ev_old + [evo]
                slab_pos[0] += 1
                n += 1

        xbuf_free = [[ev_cast], [ev_cast]]
        ybuf_free = [[ev_cast], [ev_cast]]
        a_slot_ev = [None] * 4
        spv_ev = [None, None]
        dls_ev = [None, None]
        gates_T_ev = None
        RT_evac_wait = None
        scan_done_prev = None
        BC_free = None
        Bbc_free = None
        psout_free = None
        xn1T_store_ev = None
        g_store_ev = None
        mstore_ev = [[] for _ in range(NCH)]
        wout_T_prev = None
        gt_store_prev = None
        bcrow_free = [None]
        fj = [0]

        fstate = dict(xn1Tf_free=None, gf_free=None, gh_free=None,
                      pcx_free=None, pcx1_free=None, mix_read_done=None,
                      last_out_ev=None, gtf_free=None)
        gbcf_free = [None, None]
        relu_free = [None, None]
        w2c_free = [None] * 8
        mixtmp_free = [None] * 4
        mxp = [0]
        mix_acc_evs = []
        fp_free = {6: None, 7: None}
        fp_ptr = [0]

        def next_fp():
            i2 = 6 + (fp_ptr[0] % 2)
            fp_ptr[0] += 1
            return i2

        fs = {"j": 0, "stage": 0, "evxT": None, "evg": None, "mix_init": None}

        def f_loads(j):
            t0 = j * TF
            xn1Tf_free = fstate["xn1Tf_free"]
            gf_free = fstate["gf_free"]
            dep = mstore_ev[2 * j] + mstore_ev[2 * j + 1]
            evxT = None
            for kk in range(8):
                evxT = dma(sy, DS("xf"), xn1Tf[kk][:, :],
                           xn1T_s[kk * 128:(kk + 1) * 128, t0:t0 + TF],
                           wait=dep + [xn1Tf_free] if kk == 0 else None)
            evg = dma(sy, DS("gf"), g_f[:, :], g_s[:, t0:t0 + TF], wait=gf_free)
            evgt = None
            for tb in range(NTB):
                evgt = dma(sy, DS("gtf"), gtf[:, tb * 8:(tb + 1) * 8],
                           gt_s[t0 + tb * 128: t0 + (tb + 1) * 128, :],
                           wait=fstate["gtf_free"] if tb == 0 else None)
            fs["evxT"], fs["evg"], fs["evgt_s"] = evxT, evg, evgt
            mix_acc_evs.clear()
            mix_read_done = fstate["mix_read_done"]
            for grp in range(NTB * 2):
                tb, dh = grp // 2, grp % 2
                pbm = next_fp()
                bb.W(te, fp_free[pbm])
                bb.W(te, evg)
                ins = te.matmul(PB[pbm][:, 0:512],
                                g_f[:, tb * 128:(tb + 1) * 128],
                                b2b[:, dh * 512:(dh + 1) * 512],
                                start=True, stop=True)
                ev_t = T.inc(ins)
                dst = mixed[tb][:, dh * 512:(dh + 1) * 512]
                sl = mxp[0] % 4
                mxp[0] += 1
                mt = mixtmp[sl]
                bb.W(se, ev_t)
                bb.W(se, mixtmp_free[sl])
                ins = se.activation(mt[:, :], PB[pbm][:, 0:512], AF.Copy)
                ev_s = S.inc(ins)
                fp_free[pbm] = ev_s
                ev_d = dma(ge, DS(f"mxt{sl}"), dst, mt[:, :],
                           wait=[ev_s, mix_read_done])
                mixtmp_free[sl] = ev_d
                mix_acc_evs.append(ev_d)

        def f_expert(j, e):
            t0 = j * TF
            evxT = fs["evxT"]
            gh_free = fstate["gh_free"]
            for half in range(2):
                evw2 = []
                for s8 in range(8):
                    fk = half * 8 + s8
                    ev = dma(sy, DS(f"w2c{s8}"), w2c[s8][:, :], w2b_s[e * 16 + fk, :, :],
                             wait=[w2c_free[s8]] + piece_ev[("w2", e, fk)])
                    evw2.append(ev)
                ev_gh_all = None
                for s8 in range(8):
                    fk = half * 8 + s8
                    bb.W(sy, piece_ev[("w1", e, fk)])
                    r, evw = ring_load(w1b_s[e * 16 + fk, :, :])
                    pbh = next_fp()
                    bb.W(te, fp_free[pbh])
                    bb.W(te, evw)
                    bb.W(te, evxT)
                    for kk in range(8):
                        ins = te.matmul(PB[pbh][:, 0:TF],
                                        ring[r][:, kk * 128:(kk + 1) * 128],
                                        xn1Tf[kk][:, :],
                                        start=(kk == 0), stop=(kk == 7))
                    ev_t = T.inc(ins)
                    ring_free[r] = ev_t
                    bb.W(se, ev_t)
                    bb.W(se, gh_free)
                    ins = se.activation(gh[s8][:, :], PB[pbh][:, 0:TF], AF.Relu,
                                        bias=b1_sb[:, e * 16 + fk: e * 16 + fk + 1])
                    ev_rl = S.inc(ins)
                    fp_free[pbh] = ev_rl
                    ev_gh_all = ev_rl
                for grp in range(NTB * 2):
                    tb, dh = grp // 2, grp % 2
                    pbm = next_fp()
                    bb.W(te, fp_free[pbm])
                    bb.W(te, ev_gh_all)
                    bb.W(te, evw2)
                    for s8 in range(8):
                        ins = te.matmul(PB[pbm][:, 0:512],
                                        gh[s8][:, tb * 128:(tb + 1) * 128],
                                        w2c[s8][:, dh * 512:(dh + 1) * 512],
                                        start=(s8 == 0), stop=(s8 == 7))
                    ev_t = T.inc(ins)
                    dst = mixed[tb][:, dh * 512:(dh + 1) * 512]
                    sl = mxp[0] % 4
                    mxp[0] += 1
                    mt = mixtmp[sl]
                    bb.W(se, ev_t)
                    bb.W(se, mixtmp_free[sl])
                    bb.W(se, fs["evgt_s"])
                    ins = se.activation(mt[:, :], PB[pbm][:, 0:512], AF.Identity,
                                        scale=gtf[:, tb * 8 + e: tb * 8 + e + 1])
                    ev_s = S.inc(ins)
                    fp_free[pbm] = ev_s
                    bb.W(ge, ev_s)
                    ins = ge.dma_start(out=dst, in_=mt[:, :], accum_op=OP.add)
                    ev_d = DS(f"mxt{sl}").inc(ins, 16)
                    mixtmp_free[sl] = ev_d
                    mix_acc_evs.append(ev_d)
                gh_free = (T, T.n)
                fstate["gh_free"] = gh_free
                for s8 in range(8):
                    w2c_free[s8] = (T, T.n)
            bcrow_free[0] = bcrow_free[0]
            if e == NE - 1:
                fstate["xn1Tf_free"] = (T, T.n)
                fstate["gf_free"] = (T, T.n)
                fstate["gtf_free"] = (V, V.n)

        def f_out(j):
            t0 = j * TF
            pcx_free = fstate["pcx_free"]
            pcx1_free = fstate["pcx1_free"]
            ev_mix_all = [(dsm[f"mxt{sl2}"], dsm[f"mxt{sl2}"].n) for sl2 in range(4)]
            if j == 0:
                dbg_dump("mix0", mixed[0][:, :], ev_mix_all)
            for tb in range(NTB):
                tt0 = t0 + tb * 128
                evx = dma(sy, DS("pcx"), xn1pc[:, :], xn1_s[tt0:tt0 + 128, :], wait=pcx_free)
                evx1 = dma(sy, DS("pcx1"), x1pc[:, :], x1_s[tt0:tt0 + 128, :], wait=pcx1_free)
                c = lambda jj: fcol[:, jj:jj + 1]
                bb.W(ve, ev_mix_all)
                bb.W(ve, evx)
                ins = ve.tensor_tensor(mixed[tb][:, :], mixed[tb][:, :], xn1pc[:, :], op=OP.add)
                ev_y = V.inc(ins)
                ev_red = V.inc(ve.reduce_sum(c(0), mixed[tb][:, :], axis=AX.X))
                bb.W(se, ev_y)
                se.activation(junk[:, :], mixed[tb][:, :], AF.Square, accum_out=c(1))
                ins = se.activation(c(1), c(1), AF.Copy)
                ev_sq = S.inc(ins)
                bb.W(ve, ev_red)
                vf(ve.tensor_scalar(c(2), c(0), 1.0 / D, None, op0=OP.mult))
                bb.W(ve, ev_sq)
                vf(ve.tensor_scalar(c(3), c(1), 1.0 / D, None, op0=OP.mult))
                vf(ve.tensor_tensor(c(4), c(2), c(2), op=OP.mult))
                ins = ve.tensor_tensor(c(3), c(3), c(4), op=OP.subtract)
                ev_var = V.inc(ins)
                bb.W(se, ev_var)
                sf(se.activation(c(4), c(3), AF.Ln, bias=EPS))
                ins = se.activation(c(4), c(4), AF.Exp, scale=-0.5)
                ev_r4 = S.inc(ins)
                ta = t_a[tb % 2]
                bb.W(ve, ev_r4)
                bb.W(ve, stage_free[tb % 2] if j == 0 else None)
                ve.tensor_scalar(ta[:, :], mixed[tb][:, :], c(2), c(4),
                                 op0=OP.subtract, op1=OP.mult)
                ve.tensor_tensor(ta[:, :], ta[:, :], gam_bc[:, :], op=OP.mult)
                ve.tensor_tensor(ta[:, :], ta[:, :], bet_bc[:, :], op=OP.add)
                bb.W(ve, evx1)
                ins = ve.tensor_tensor(xn1pc[:, :], ta[:, :], x1pc[:, :], op=OP.add)
                ev_out = V.inc(ins)
                pcx1_free = ev_out
                fstate["pcx1_free"] = ev_out
                ev_st = dma(sy, DS("out"), out_h[tt0:tt0 + 128, :], xn1pc[:, :], wait=ev_out)
                pcx_free = ev_st
                fstate["pcx_free"] = ev_st
                fstate["last_out_ev"] = ev_st
            fstate["mix_read_done"] = (V, V.n)

        TOT_STAGES = NFC * (NE + 1)

        def fpump(k_done, budget=1, pace=True, frac=0.0):
            while budget > 0:
                j, st = fs["j"], fs["stage"]
                if j >= NFC or 2 * j + 1 > k_done:
                    return
                done = j * (NE + 1) + st
                if pace and k_done < NCH - 1:
                    target = TOT_STAGES * (k_done + frac) / (NCH - 2.0)
                    if done >= target:
                        return
                if st < NE:
                    need = 32 * (st + 1)
                    if slab_pos[0] < need:
                        emit_slabs(need - slab_pos[0])
                    if st == 0:
                        f_loads(j)
                    f_expert(j, st)
                    fs["stage"] += 1
                else:
                    f_out(j)
                    fs["j"] += 1
                    fs["stage"] = 0
                budget -= 1


        for k in range(NCH):
            t0 = k * TC
            par = k % 2
            emit_slabs(24)
            xev = []
            for tt in range(2):
                ev = dma(sy, DS(f"xb{tt}"), xbuf[tt][:, :],
                         x_h[t0 + tt * 128: t0 + (tt + 1) * 128, :],
                         wait=xbuf_free[tt])
                xev.append(ev)

            if k == 0:
                dbg_dump("xb0", xbuf[0][:, :], xev[0])
            ev_h0 = None
            for tt in range(2):
                c = lambda j: scol[:, par * 16 + tt * 8 + j: par * 16 + tt * 8 + j + 1]
                bb.W(ve, xev[tt])
                ev_red = V.inc(ve.reduce_sum(c(0), xbuf[tt][:, :], axis=AX.X))
                bb.W(se, xev[tt])
                se.activation(junk[:, :], xbuf[tt][:, :], AF.Square, accum_out=c(1))
                ins = se.activation(c(1), c(1), AF.Copy)
                ev_sq = S.inc(ins)
                bb.W(ve, ev_red)
                vf(ve.tensor_scalar(c(2), c(0), 1.0 / D, None, op0=OP.mult))
                bb.W(ve, ev_sq)
                vf(ve.tensor_scalar(c(3), c(1), 1.0 / D, None, op0=OP.mult))
                vf(ve.tensor_tensor(c(4), c(2), c(2), op=OP.mult))
                ins = ve.tensor_tensor(c(3), c(3), c(4), op=OP.subtract)
                ev_var = V.inc(ins)
                bb.W(se, ev_var)
                sf(se.activation(c(4), c(3), AF.Ln, bias=EPS))
                sf(se.activation(c(4), c(4), AF.Exp, scale=-0.5))
                sf(se.activation(c(5), c(3), AF.Ln, scale=1.0 + EPS, bias=EPS * EPS))
                ins = se.activation(c(5), c(5), AF.Exp, scale=-0.5)
                ev_r = S.inc(ins)
                bb.W(ve, ev_r)
                bb.W(ve, ybuf_free[tt])
                ve.tensor_scalar(ybuf[tt][:, :], xbuf[tt][:, :], c(2), c(5),
                                 op0=OP.subtract, op1=OP.mult)
                bb.W(ve, ev_bcast)
                ve.tensor_tensor(ybuf[tt][:, :], ybuf[tt][:, :], g_bc[:, :], op=OP.mult)
                ve.tensor_tensor(ybuf[tt][:, :], ybuf[tt][:, :], bb_bc[:, :], op=OP.add)
                vf(ve.tensor_scalar(c(6), c(4), 1.0, None, op0=OP.add))
                vf(ve.tensor_tensor(c(7), c(2), c(4), op=OP.mult))
                ins = ve.tensor_scalar(xbuf[tt][:, :], xbuf[tt][:, :], c(6), c(7),
                                        op0=OP.mult, op1=OP.subtract)
                ev_h0 = V.inc(ins)
            ev_ln = ev_h0

            tr_jobs = ([("R", tt, dc) for tt in range(2) for dc in range(8)]
                       + [("H", tt, dc) for tt in range(2) for dc in range(8)])
            ev_evacR = None
            ev_evacH = None
            for g0 in range(0, 32, 4):
                pb = next_tr()
                bb.W(te, tr_free[pb])
                bb.W(te, ev_ln)
                bb.W(te, ev_g0)
                for jj in range(4):
                    kind, tt, dc = tr_jobs[g0 + jj]
                    src = xbuf[tt] if kind == "R" else ybuf[tt]
                    ins = te.transpose(PB[pb][:, jj * 128:(jj + 1) * 128],
                                       src[:, dc * 128:(dc + 1) * 128], idf[:, :])
                ev_t = T.inc(ins)
                bb.W(se, ev_t)
                bb.W(se, RT_evac_wait)
                for jj in range(4):
                    kind, tt, dc = tr_jobs[g0 + jj]
                    dst = (RT[dc] if kind == "R" else h0T[dc])[:, tt * 128:(tt + 1) * 128]
                    ins = se.activation(dst, PB[pb][:, jj * 128:(jj + 1) * 128], AF.Copy)
                ev_s = S.inc(ins)
                tr_free[pb] = ev_s
                if g0 < 16:
                    ev_evacR = ev_s
                ev_evacH = ev_s
            ev_R_T_done = None

            xin_ev = [None] * 16
            for m in range(32):
                r, evw = ring_load(winb_s[m, :, :])
                if k == 0:
                    bb.W(sy, ev_prep_all)
                pb = next_aux()
                bb.W(te, aux_free[pb])
                bb.W(te, evw)
                bb.W(te, ev_evacH)
                for kk in range(8):
                    ins = te.matmul(PB[pb][:, 0:TC], ring[r][:, kk * 128:(kk + 1) * 128],
                                    h0T[kk][:, :], start=(kk == 0), stop=(kk == 7))
                ev_t = T.inc(ins)
                ring_free[r] = ev_t
                bb.W(se, ev_t)
                bb.W(se, scan_done_prev)
                bb.W(se, wout_T_prev)
                if m < 16:
                    ins = se.activation(xin[m][:, 3:3 + TC], PB[pb][:, 0:TC], AF.Copy)
                else:
                    ins = se.activation(zsil[m - 16][:, :], PB[pb][:, 0:TC], AF.Silu)
                ev_s = S.inc(ins)
                aux_free[pb] = ev_s
                if m < 16:
                    xin_ev[m] = ev_s
            ev_xin = (S, S.n)
            fpump(k - 1, frac=0.25, budget=2)
            if k == 0:
                dbg_dump("sc0", scol[:, 0:16], ev_ln)
                dbg_dump("yb0", ybuf[0][:, :], ev_ln)
                dbg_dump("h0T0", h0T[0][:, :], ev_evacH)
                dbg_dump("RT0", RT[0][:, :], ev_evacR)
                dbg_dump("zsil0", zsil[0][:, :], ev_xin)
                dbg_dump("A0", A_sb[:, :], ev_vA)

            ev_xc = None
            silu_ev = [None, None]
            for i in range(16):
                ca = cacc[i % 2]
                bb.W(ve, xin_ev[i])
                bb.W(ve, silu_ev[i % 2])
                ve.tensor_scalar(ca[:, :], xin[i][:, 0:TC], cw_sb[:, 4 * i:4 * i + 1],
                                 None, op0=OP.mult)
                for j in range(1, 4):
                    ins = ve.scalar_tensor_tensor(ca[:, :], in0=xin[i][:, j:j + TC],
                                                  scalar=cw_sb[:, 4 * i + j:4 * i + j + 1],
                                                  in1=ca[:, :], op0=OP.mult, op1=OP.add)
                ev_cv = V.inc(ins)
                bb.W(ge, ev_cv)
                ins = ge.tensor_copy(xin[i][:, 0:3], xin[i][:, TC:TC + 3])
                ev_gt = G.inc(ins)
                bb.W(se, ev_cv)
                bb.W(se, ev_gt)
                ins = se.activation(xin[i][:, 3:3 + TC], ca[:, :], AF.Silu,
                                    bias=cb_sb[:, i:i + 1])
                ev_xc = S.inc(ins)
                silu_ev[i % 2] = ev_xc
            ev_xc_all = ev_xc
            fpump(k - 1, frac=0.35, budget=2)
            if k == 0:
                dbg_dump("xin0", xin[0][:, 0:TC + 3], ev_xc_all)
                dbg_dump("xpb", xprojb[:, :], ev_cast)
                if dbg:
                    for i2 in range(16):
                        bb.W(sy, ev_xc_all)
                        ins = sy.dma_start(out=dbg_outs["xcall"][:, i2 * TC:(i2 + 1) * TC],
                                           in_=xin[i2][:, 3:3 + TC])
                        DS("dbg").inc(ins, 16)

            pb_dbc = next_aux()
            bb.W(te, aux_free[pb_dbc])
            bb.W(te, ev_xc_all)
            bb.W(te, ev_cast)
            for kk in range(16):
                ins = te.matmul(PB[pb_dbc][0:96, 0:TC], xprojb[:, kk * 96:(kk + 1) * 96],
                                xin[kk][:, 3:3 + TC], start=(kk == 0), stop=(kk == 15))
            ev_dbc = T.inc(ins)
            bb.W(se, ev_dbc)
            ins = se.activation(dt_b[:, :], PB[pb_dbc][0:RK, 0:TC], AF.Copy)
            ev_dt = S.inc(ins)
            bb.W(ve, ev_dbc)
            bb.W(ve, ev_dt)
            bb.W(ve, BC_free)
            ins = ve.tensor_copy(BC_sb[64:96, :], PB[pb_dbc][64:96, 0:TC])
            ev_bc_ev = V.inc(ins)
            aux_free[pb_dbc] = ev_bc_ev
            bcn_ev = [None] * 16
            ccn_ev = [None] * 16
            for cc in range(2):
                ev_bcrow = dma(sy, DS("bcrow"), BCrow[0:1, :],
                               BC_sb[64 + cc * 16:80 + cc * 16, :],
                               wait=[ev_bc_ev, bcrow_free[0]])
                for n in range(16):
                    pb = next_aux()
                    bb.W(te, aux_free[pb])
                    bb.W(te, ev_bcrow)
                    ins = te.matmul(PB[pb][:, 0:TC], ones_bf[0:1, :],
                                    BCrow[0:1, n * TC:(n + 1) * TC], start=True, stop=True)
                    ev_t = T.inc(ins)
                    dst = B_bc[n] if cc == 0 else C_bc[n]
                    bb.W(se, ev_t)
                    bb.W(se, scan_done_prev)
                    ins = se.activation(dst[:, :], PB[pb][:, 0:TC], AF.Copy)
                    ev_s = S.inc(ins)
                    aux_free[pb] = ev_s
                    (bcn_ev if cc == 0 else ccn_ev)[n] = ev_s
                bcrow_free[0] = (T, T.n)
            BC_done = (S, S.n)
            BC_free = ev_bcrow
            fpump(k - 1, frac=0.45, budget=2)
            if k == 0:
                dbg_dump("Bbc0", B_bc[0][:, :], BC_done)
                dbg_dump("Cbc0", C_bc[0][:, :], BC_done)
                dbg_dump("dtb0", dt_b[:, :], ev_dt)

            dl_s = [None, None]
            ev_dl_p = [None, None]

            def delta_chain(i):
                pb = next_aux()
                bb.W(te, aux_free[pb])
                bb.W(te, ev_dt)
                bb.W(te, ev_cast)
                ins = te.matmul(PB[pb][:, 0:TC], wdtb[:, i * 128:(i + 1) * 128],
                                dt_b[:, :], start=True, stop=True)
                ev_t = T.inc(ins)
                sp = sp_t[i % 2]
                bb.W(se, ev_t)
                bb.W(se, spv_ev[i % 2])
                se.activation(sp[:, :], PB[pb][:, 0:TC], AF.Abs, bias=bdt_sb[:, i:i + 1])
                se.activation(sp[:, :], sp[:, :], AF.Exp, scale=-1.0)
                ins = se.activation(sp[:, :], sp[:, :], AF.Ln, bias=1.0)
                ev_sp = S.inc(ins)
                dl = dlt[i % 2]
                bb.W(ve, ev_sp)
                bb.W(ve, dls_ev[i % 2])
                ve.scalar_tensor_tensor(dl[:, :], in0=PB[pb][:, 0:TC],
                                        scalar=bdt_sb[:, i:i + 1], in1=sp[:, :],
                                        op0=OP.add, op1=OP.add)
                ins = ve.tensor_tensor(dl[:, :], dl[:, :], sp[:, :], op=OP.max)
                ev_dl = V.inc(ins)
                aux_free[pb] = ev_dl
                spv_ev[i % 2] = ev_dl
                d_u = du[i % 2]
                ins = ve.tensor_tensor(d_u[:, :], dl[:, :], xin[i][:, 3:3 + TC], op=OP.mult)
                V.inc(ins)
                ev_dl_p[i % 2] = ev_dl
                if k == 0 and i == 0:
                    dbg_dump("dlt0", dl[:, :], ev_dl)

            delta_chain(0)
            for i in range(16):
                dl = dlt[i % 2]
                d_u = du[i % 2]
                bb.W(se, ev_dl_p[i % 2])
                ev_scan = None
                scan_ev11 = None
                for n in range(NST):
                    sl = a_t[n % 4]
                    bb.W(se, a_slot_ev[n % 4])
                    ins = se.activation(sl[:, :], dl[:, :], AF.Exp,
                                        scale=A_sb[:, i * 16 + n: i * 16 + n + 1])
                    a_ev = S.inc(ins)
                    bbuf = b_t[n % 2]
                    ss = s_all[:, n * TC:(n + 1) * TC]
                    bb.W(ve, bcn_ev[n])
                    ve.tensor_tensor(bbuf[:, :], d_u[:, :], B_bc[n][:, :], op=OP.mult)
                    bb.W(ve, a_ev)
                    ins = ve.tensor_tensor_scan(
                        ss, sl[:, :], bbuf[:, :],
                        initial=s_state[:, i * 16 + n: i * 16 + n + 1],
                        op0=OP.mult, op1=OP.add)
                    ev_scan = V.inc(ins)
                    a_slot_ev[n % 4] = ev_scan
                dls_ev[i % 2] = (S, S.n)
                if i + 1 < 16:
                    delta_chain(i + 1)
                bb.W(ve, ev_scan)
                ve.tensor_copy(s_state[:, i * 16:(i + 1) * 16],
                               s_all[:, TC - 1::TC])
                bb.W(ve, ccn_ev[15])
                for n in range(NST):
                    ve.tensor_tensor(s_all[:, n * TC:(n + 1) * TC],
                                     s_all[:, n * TC:(n + 1) * TC],
                                     C_bc[n][:, :], op=OP.mult)
                for step in (1, 2, 4, 8):
                    for base in range(0, NST, 2 * step):
                        ve.tensor_tensor(s_all[:, base * TC:(base + 1) * TC],
                                         s_all[:, base * TC:(base + 1) * TC],
                                         s_all[:, (base + step) * TC:(base + step + 1) * TC],
                                         op=OP.add)
                ve.scalar_tensor_tensor(yacc[:, :], in0=xin[i][:, 3:3 + TC],
                                        scalar=dp_sb[:, i:i + 1],
                                        in1=s_all[:, 0:TC],
                                        op0=OP.mult, op1=OP.add)
                ins = ve.tensor_tensor(xin[i][:, 3:3 + TC], yacc[:, :],
                                       zsil[i][:, :], op=OP.mult)
                ev_yg = V.inc(ins)
                if k == 0 and i == 0:
                    dbg_dump("yg0", xin[i][:, 3:3 + TC], ev_yg)
                if i % 4 == 3:
                    fpump(k - 1, frac=0.45 + (i + 1) * 0.03, budget=2)
            scan_done_prev = (V, V.n)
            ev_yg_all = scan_done_prev

            bb.W(ve, ev_evacR)
            for dh in range(2):
                for i in range(16):
                    r, evw = ring_load(woutb_s[i, :, :])
                    bb.W(te, evw)
                    bb.W(te, ev_yg_all)
                    if i == 0:
                        bb.W(te, psout_free)
                    for m in range(4):
                        mm = dh * 4 + m
                        pbo = 4 + m // 2
                        ins = te.matmul(PB[pbo][:, (m % 2) * TC:(m % 2 + 1) * TC],
                                        ring[r][:, mm * 128:(mm + 1) * 128],
                                        xin[i][:, 3:3 + TC],
                                        start=(i == 0 and m % 2 == 0),
                                        stop=(i == 15 and m % 2 == 1))
                    ev_wout_last = T.inc(ins)
                    ring_free[r] = ev_wout_last
                bb.W(ve, ev_wout_last)
                for m in range(4):
                    mm = dh * 4 + m
                    pbo = 4 + m // 2
                    ins = ve.tensor_tensor(RT[mm][:, :], RT[mm][:, :],
                                           PB[pbo][:, (m % 2) * TC:(m % 2 + 1) * TC],
                                           op=OP.add)
                psout_free = V.inc(ins)
            ev_x1T = psout_free
            wout_T_prev = (T, T.n)
            fpump(k - 1, frac=0.97, budget=2)

            ev_x1tok = None
            for g0 in range(0, 16, 4):
                pb = next_tr()
                bb.W(te, tr_free[pb])
                bb.W(te, ev_x1T)
                for jj in range(4):
                    idx = g0 + jj
                    tt, dc = idx // 8, idx % 8
                    ins = te.transpose(PB[pb][:, jj * 128:(jj + 1) * 128],
                                       RT[dc][:, tt * 128:(tt + 1) * 128], idf[:, :])
                ev_t = T.inc(ins)
                bb.W(se, ev_t)
                for jj in range(4):
                    idx = g0 + jj
                    tt, dc = idx // 8, idx % 8
                    ins = se.activation(xbuf[tt][:, dc * 128:(dc + 1) * 128],
                                        PB[pb][:, jj * 128:(jj + 1) * 128], AF.Copy)
                ev_s = S.inc(ins)
                tr_free[pb] = ev_s
                ev_x1tok = ev_s
            RT_evac_wait = (T, T.n)

            ev_xn1 = None
            for tt in range(2):
                c = lambda j: scol[:, par * 16 + tt * 8 + j: par * 16 + tt * 8 + j + 1]
                bb.W(ve, ev_x1tok)
                ev_red = V.inc(ve.reduce_sum(c(0), xbuf[tt][:, :], axis=AX.X))
                bb.W(se, ev_x1tok)
                se.activation(junk[:, :], xbuf[tt][:, :], AF.Square, accum_out=c(1))
                ins = se.activation(c(1), c(1), AF.Copy)
                ev_sq = S.inc(ins)
                bb.W(ve, ev_red)
                vf(ve.tensor_scalar(c(2), c(0), 1.0 / D, None, op0=OP.mult))
                bb.W(ve, ev_sq)
                vf(ve.tensor_scalar(c(3), c(1), 1.0 / D, None, op0=OP.mult))
                vf(ve.tensor_tensor(c(4), c(2), c(2), op=OP.mult))
                ins = ve.tensor_tensor(c(3), c(3), c(4), op=OP.subtract)
                ev_var = V.inc(ins)
                bb.W(se, ev_var)
                sf(se.activation(c(4), c(3), AF.Ln, bias=EPS))
                ins = se.activation(c(4), c(4), AF.Exp, scale=-0.5)
                ev_r3 = S.inc(ins)
                bb.W(ve, ev_r3)
                bb.W(ve, ybuf_free[tt])
                ins = ve.tensor_scalar(ybuf[tt][:, :], xbuf[tt][:, :], c(2), c(4),
                                        op0=OP.subtract, op1=OP.mult)
                ev_xn1 = V.inc(ins)
                ev_st1 = dma(sy, DS(f"xb{tt}"), x1_s[t0 + tt * 128: t0 + (tt + 1) * 128, :],
                             xbuf[tt][:, :], wait=ev_x1tok)
                ev_st2 = dma(sy, DS(f"yb{tt}"), xn1_s[t0 + tt * 128: t0 + (tt + 1) * 128, :],
                             ybuf[tt][:, :], wait=ev_xn1)
                xbuf_free[tt] = [ev_st1]
            ev_xn1_all = ev_xn1

            ev_xn1T = None
            for g0 in range(0, 16, 4):
                pb = next_tr()
                bb.W(te, tr_free[pb])
                bb.W(te, ev_xn1_all)
                for jj in range(4):
                    idx = g0 + jj
                    tt, dc = idx // 8, idx % 8
                    ins = te.transpose(PB[pb][:, jj * 128:(jj + 1) * 128],
                                       ybuf[tt][:, dc * 128:(dc + 1) * 128], idf[:, :])
                ev_t = T.inc(ins)
                bb.W(se, ev_t)
                bb.W(se, xn1T_store_ev)
                for jj in range(4):
                    idx = g0 + jj
                    tt, dc = idx // 8, idx % 8
                    ins = se.activation(xn1T[dc][:, tt * 128:(tt + 1) * 128],
                                        PB[pb][:, jj * 128:(jj + 1) * 128], AF.Copy)
                ev_s = S.inc(ins)
                tr_free[pb] = ev_s
                ev_xn1T = ev_s
            for tt in range(2):
                ybuf_free[tt] = [(T, T.n), (dsm[f"yb{tt}"], dsm[f"yb{tt}"].n)]
            ev_xn1T_st = None
            for dc in range(8):
                ev_xn1T_st = dma(sy, DS("xt"), xn1T_s[dc * 128:(dc + 1) * 128, t0:t0 + TC],
                                 xn1T[dc][:, :], wait=ev_xn1T)
            xn1T_store_ev = ev_xn1T_st

            pb = next_aux()
            bb.W(te, aux_free[pb])
            bb.W(te, ev_xn1T)
            for kk in range(8):
                ins = te.matmul(PB[pb][0:NE, 0:TC], wgb[:, kk * 8:(kk + 1) * 8],
                                xn1T[kk][:, :], start=(kk == 0), stop=(kk == 7))
            ev_t = T.inc(ins)
            bb.W(se, ev_t)
            ins = se.activation(lg_sb[:, :], PB[pb][0:NE, 0:TC], AF.Identity,
                                bias=bg_c[:, :])
            ev_lg = S.inc(ins)
            aux_free[pb] = ev_lg
            pbt = next_aux()
            bb.W(te, aux_free[pbt])
            bb.W(te, ev_lg)
            for tt in range(2):
                ins = te.transpose(PB[pbt][0:128, tt * 8:(tt + 1) * 8],
                                   lg_sb[:, tt * 128:(tt + 1) * 128], idf[0:NE, 0:NE])
            ev_t = T.inc(ins)
            bb.W(ve, ev_t)
            for tt in range(2):
                vf(ve.tensor_copy(lgt[tt][:, :], PB[pbt][0:128, tt * 8:(tt + 1) * 8]))
            ev_lgt = (V, V.n)
            aux_free[pbt] = ev_lgt
            ev_gts = None
            bb.W(ve, gates_T_ev)
            bb.W(ve, gt_store_prev)
            for tt in range(2):
                c = lambda j: gcol[:, par * 16 + tt * 8 + j: par * 16 + tt * 8 + j + 1]
                ev_rm1 = V.inc(ve.reduce_max(c(0), lgt[tt][:, :], axis=AX.X))
                bb.W(ve, ev_rm1)
                vf(ve.tensor_scalar(eq1[tt][:, :], lgt[tt][:, :], c(0), None, op0=OP.is_ge))
                vf(ve.scalar_tensor_tensor(msk[tt][:, :], in0=eq1[tt][:, :], scalar=-1e30,
                                           in1=lgt[tt][:, :], op0=OP.mult, op1=OP.add))
                ev_rm2 = V.inc(ve.reduce_max(c(1), msk[tt][:, :], axis=AX.X))
                bb.W(ve, ev_rm2)
                ins = ve.tensor_tensor(c(2), c(1), c(0), op=OP.subtract)
                ev_d12 = V.inc(ins)
                bb.W(se, ev_d12)
                ins = se.activation(c(3), c(2), AF.Exp)
                ev_e = S.inc(ins)
                bb.W(ve, ev_e)
                vf(ve.tensor_scalar(c(4), c(3), 1.0, None, op0=OP.add))
                vf(ve.reciprocal(c(4), c(4)))
                vf(ve.tensor_scalar(c(5), c(4), -1.0, 1.0, op0=OP.mult, op1=OP.add))
                vf(ve.tensor_scalar(msk[tt][:, :], msk[tt][:, :], c(1), None, op0=OP.is_ge))
                vf(ve.tensor_scalar(gts[tt][:, :], eq1[tt][:, :], c(4), None, op0=OP.mult))
                ins = ve.scalar_tensor_tensor(gts[tt][:, :], in0=msk[tt][:, :], scalar=c(5),
                                              in1=gts[tt][:, :], op0=OP.mult, op1=OP.add)
                ev_gts = V.inc(ins)
            pbt = next_aux()
            bb.W(te, aux_free[pbt])
            bb.W(te, ev_gts)
            for tt in range(2):
                ins = te.transpose(PB[pbt][0:NE, tt * 128:(tt + 1) * 128],
                                   gts[tt][:, :], idf[:, :])
            ev_t = T.inc(ins)
            gates_T_ev = ev_t
            bb.W(se, ev_t)
            bb.W(se, g_store_ev)
            for tt in range(2):
                ins = se.activation(g_sb[:, tt * 128:(tt + 1) * 128],
                                    PB[pbt][0:NE, tt * 128:(tt + 1) * 128], AF.Copy)
            ev_gsb = S.inc(ins)
            aux_free[pbt] = ev_gsb
            g_store_ev = dma(sy, DS("g"), g_s[:, t0:t0 + TC], g_sb[:, :], wait=ev_gsb)
            ev_gt_st = None
            for tt in range(2):
                ev_gt_st = dma(sy, DS("gt"), gt_s[t0 + tt * 128: t0 + (tt + 1) * 128, :],
                               gts[tt][:, :], wait=ev_gts if tt == 1 else None)
            gt_store_prev = ev_gt_st
            mstore_ev[k] += [xbuf_free[0][0], xbuf_free[1][0],
                             (dsm["yb0"], dsm["yb0"].n), (dsm["yb1"], dsm["yb1"].n),
                             (dsm["xt"], dsm["xt"].n), g_store_ev, ev_gt_st]
            fpump(k, budget=2)


        while fs["j"] < NFC:
            fpump(NCH - 1, budget=100, pace=False)

        bb.W(sy, fstate["last_out_ev"])
        bb.W(sy, (dsm["out"], dsm["out"].n))

    return nc


_NC_CACHE = {}


def _get_nc():
    if "nc" not in _NC_CACHE:
        _NC_CACHE["nc"] = build_nc()
    return _NC_CACHE["nc"]


def kernel(**inputs):
    g = lambda n: np.asarray(inputs[n])
    x = g("L0_attn_x")
    f32 = np.float32

    common = {
        "w_in": np.ascontiguousarray(g("L0_attn_W_in"), dtype=f32),
        "w_out": np.ascontiguousarray(g("L0_attn_W_out"), dtype=f32),
        "w_xp": np.ascontiguousarray(g("L0_attn_W_xproj"), dtype=f32),
        "w_dt": np.ascontiguousarray(g("L0_attn_W_dt"), dtype=f32),
        "w1": np.ascontiguousarray(g("L0_ffn_W1"), dtype=f32),
        "w2": np.ascontiguousarray(g("L0_ffn_W2"), dtype=f32),
        "wg": np.ascontiguousarray(g("L0_ffn_Wg"), dtype=f32),
        "lng": np.ascontiguousarray(g("L0_attn_ln_g").reshape(1, D), dtype=f32),
        "lnb": np.ascontiguousarray(g("L0_attn_ln_b").reshape(1, D), dtype=f32),
        "gam": np.ascontiguousarray(g("L0_ffn_gamma").reshape(1, D), dtype=f32),
        "bet": np.ascontiguousarray(g("L0_ffn_beta").reshape(1, D), dtype=f32),
        "alog": np.ascontiguousarray(
            g("L0_attn_A_log").reshape(16, 128, NST).transpose(1, 0, 2).reshape(128, 16 * NST),
            dtype=f32),
        "cw": np.ascontiguousarray(
            g("L0_attn_conv_w").reshape(DI, 4).reshape(16, 128, 4)
            .transpose(1, 0, 2).reshape(128, 64), dtype=f32),
        "cb": np.ascontiguousarray(g("L0_attn_conv_b").reshape(16, 128).T, dtype=f32),
        "bdt": np.ascontiguousarray(g("L0_attn_b_dt").reshape(16, 128).T, dtype=f32),
        "dpar": np.ascontiguousarray(g("L0_attn_D_param").reshape(16, 128).T, dtype=f32),
        "b1": np.ascontiguousarray(
            g("L0_ffn_b1").reshape(NE, 16, 128).transpose(2, 0, 1).reshape(128, 128),
            dtype=f32),
        "b2": np.ascontiguousarray(g("L0_ffn_b2"), dtype=f32),
        "bg": np.ascontiguousarray(g("L0_ffn_bg").reshape(NE, 1), dtype=f32),
    }
    in_maps = [dict(common, x=np.ascontiguousarray(x[b], dtype=f32)) for b in range(8)]
    nc = _get_nc()
    res = run_bass_kernel_spmd(nc, in_maps, core_ids=list(range(8)))
    out = np.stack([np.asarray(r["out"]) for r in res.results], axis=0)
    return out.astype(np.float32)


if __name__ == "__main__":
    nc = build_nc()
    print("built ok")


# version 14
# speedup vs baseline: 1.0619x; 1.0619x over previous
import contextlib
import numpy as np

import concourse.bass as bass
import concourse.mybir as mybir
from concourse.alu_op_type import AluOpType as OP
from concourse.bass_utils import run_bass_kernel_spmd

F32 = mybir.dt.float32
BF16 = mybir.dt.bfloat16
AF = mybir.ActivationFunctionType
AX = mybir.AxisListType

L = 4096
D = 1024
DI = 2048
NST = 16
RK = 64
NE = 8
FF = 2048
TC = 256
TF = 512
NCH = L // TC
NFC = L // TF
EPS = 1e-5


class Sem:
    def __init__(self, es, nc, name):
        self.h = es.enter_context(nc.semaphore(name))
        self.n = 0

    def inc(self, ins, k=1):
        self.n += k
        ins.then_inc(self.h, k)
        return (self, self.n)

    @property
    def ev(self):
        return (self, self.n)


class B:

    def __init__(self, nc, es):
        self.nc = nc
        self.es = es
        self._waited = {}

    def sem(self, name):
        return Sem(self.es, self.nc, name)

    def sb(self, name, shape, dt):
        return self.es.enter_context(self.nc.sbuf_tensor(name, shape, dt))

    def ps(self, name, shape, dt):
        return self.es.enter_context(self.nc.psum_tensor(name, shape, dt))

    def W(self, eng, ev):
        if ev is None:
            return
        if isinstance(ev, list):
            for e in ev:
                self.W(eng, e)
            return
        sem, n = ev
        if n <= 0:
            return
        assert n <= sem.n, f"wait target {n} > emitted {sem.n}: deadlock"
        key = (id(eng), id(sem))
        if self._waited.get(key, 0) >= n:
            return
        eng.wait_ge(sem.h, n)
        self._waited[key] = n


def build_nc(dbg=False):
    nc = bass.Bass("TRN2", target_bir_lowering=False,
                   detect_race_conditions=False)
    es = contextlib.ExitStack()
    bb = B(nc, es)

    dr = lambda n, s, t, k: nc.dram_tensor(n, s, t, kind=k).ap()
    x_h = dr("x", [L, D], F32, "ExternalInput")
    out_h = dr("out", [L, D], F32, "ExternalOutput")
    w_in_h = dr("w_in", [D, 2 * DI], F32, "ExternalInput")
    w_out_h = dr("w_out", [DI, D], F32, "ExternalInput")
    w_xp_h = dr("w_xp", [DI, 96], F32, "ExternalInput")
    w_dt_h = dr("w_dt", [RK, DI], F32, "ExternalInput")
    w1_h = dr("w1", [NE, D, FF], F32, "ExternalInput")
    w2_h = dr("w2", [NE, FF, D], F32, "ExternalInput")
    wg_h = dr("wg", [D, NE], F32, "ExternalInput")
    lng_h = dr("lng", [1, D], F32, "ExternalInput")
    lnb_h = dr("lnb", [1, D], F32, "ExternalInput")
    gam_h = dr("gam", [1, D], F32, "ExternalInput")
    bet_h = dr("bet", [1, D], F32, "ExternalInput")
    alog_h = dr("alog", [128, 16 * NST], F32, "ExternalInput")
    cw_h = dr("cw", [128, 16 * 4], F32, "ExternalInput")
    cb_h = dr("cb", [128, 16], F32, "ExternalInput")
    bdt_h = dr("bdt", [128, 16], F32, "ExternalInput")
    dpar_h = dr("dpar", [128, 16], F32, "ExternalInput")
    b1_h = dr("b1", [128, 128], F32, "ExternalInput")
    b2_h = dr("b2", [NE, D], F32, "ExternalInput")
    bg_h = dr("bg", [NE, 1], F32, "ExternalInput")

    sk = "ExternalOutput" if dbg else "Internal"
    dbg_outs = {}
    def dbg_decl(name, shape, dt=F32):
        if dbg:
            dbg_outs[name] = nc.dram_tensor("dbg_" + name, shape, dt, kind="ExternalOutput").ap()
    winb_s = dr("winb_s", [32, 128, 1024], BF16, "Internal")
    woutb_s = dr("woutb_s", [16, 128, 1024], BF16, "Internal")
    w1b_s = dr("w1b_s", [NE * 16, 128, 1024], BF16, "Internal")
    w2b_s = dr("w2b_s", [NE * 16, 128, 1024], BF16, "Internal")
    x1_s = dr("x1_s", [L, D], F32, sk)
    xn1_s = dr("xn1_s", [L, D], F32, sk)
    xn1T_s = dr("xn1T_s", [D, L], BF16, "Internal")
    g_s = dr("g_s", [NE, L], BF16, sk)
    gt_s = dr("gt_s", [L, NE], F32, "Internal")

    dbg_decl("h0T0", [128, TC], BF16); dbg_decl("RT0", [128, TC])
    dbg_decl("xin0", [128, TC + 3], BF16); dbg_decl("zsil0", [128, TC], BF16)
    dbg_decl("dlt0", [128, TC]); dbg_decl("du0", [128, TC], BF16)
    dbg_decl("a0", [128, TC], BF16); dbg_decl("Bbc0", [128, TC], BF16)
    dbg_decl("Cbc0", [128, TC], BF16); dbg_decl("s0", [128, TC], BF16)
    dbg_decl("yacc0", [128, TC]); dbg_decl("yg0", [128, TC], BF16)
    dbg_decl("A0", [128, 16 * NST]); dbg_decl("dtb0", [RK, TC], BF16)
    dbg_decl("xb0", [128, D]); dbg_decl("sc0", [128, 16]); dbg_decl("yb0", [128, D])
    dbg_decl("xpb", [128, 16 * 96], BF16); dbg_decl("xcall", [128, 16 * TC], BF16)
    dbg_decl("mix0", [128, D]); dbg_decl("gh0", [128, TF], BF16); dbg_decl("gbcf0", [128, TF], BF16)
    with es:
        sb = bb.sb
        ones_bf = sb("ones_bf", [1, 128], BF16)
        ones_f = sb("ones_f", [1, 128], F32)
        idf = sb("idf", [128, 128], F32)
        g_bc = sb("g_bc", [128, D], BF16)
        bb_bc = sb("bb_bc", [128, D], BF16)
        gam_bc = sb("gam_bc", [128, D], BF16)
        bet_bc = sb("bet_bc", [128, D], BF16)
        xprojb = sb("xprojb", [128, 16 * 96], BF16)
        wdtb = sb("wdtb", [RK, DI], BF16)
        wgb = sb("wgb", [128, NE * 8], BF16)
        b2b = sb("b2b", [NE, D], BF16)
        bg_c = sb("bg_c", [NE, 1], F32)
        A_sb = sb("A_sb", [128, 16 * NST], F32)
        cw_sb = sb("cw_sb", [128, 64], F32)
        cb_sb = sb("cb_sb", [128, 16], F32)
        bdt_sb = sb("bdt_sb", [128, 16], F32)
        dp_sb = sb("dp_sb", [128, 16], F32)
        b1_sb = sb("b1_sb", [128, 128], F32)
        row1 = sb("row1", [1, D], F32)
        s_state = sb("s_state", [128, 16 * NST], F32)
        cvals = [EPS, EPS * EPS, 1.0 + EPS, 1.0, -1.0, -0.5, 0.0]
        cst = sb("cst", [128, len(cvals)], F32)
        NRING = 7
        ring = [sb(f"ring{r}", [128, 1024], BF16) for r in range(NRING)]
        xbuf = [sb(f"xbuf{t}", [128, D], F32) for t in range(2)]
        ybuf = [sb(f"ybuf{t}", [128, D], F32) for t in range(2)]
        scol = sb("scol", [128, 32], F32)
        RT = [sb(f"RT{m}", [128, TC], F32) for m in range(8)]
        h0T = [sb(f"h0T{m}", [128, TC], BF16) for m in range(8)]
        xin = [sb(f"xin{i}", [128, TC + 4], BF16) for i in range(16)]
        cacc = [sb(f"cacc{r}", [128, TC], F32) for r in range(2)]
        zsil = [sb(f"zsil{i}", [128, TC], BF16) for i in range(16)]
        dt_b = sb("dt_b", [RK, TC], BF16)
        BC_sb = sb("BC_sb", [128, TC], BF16)
        BCrow = sb("BCrow", [1, 16 * TC], BF16)
        B_bc = [sb(f"Bbc{n}", [128, TC], BF16) for n in range(NST)]
        C_bc = [sb(f"Cbc{n}", [128, TC], BF16) for n in range(NST)]
        dlt = [sb(f"dlt{r}", [128, TC], F32) for r in range(2)]
        du = [sb(f"du{r}", [128, TC], BF16) for r in range(2)]
        sp_t = [sb(f"spt{r}", [128, TC], F32) for r in range(2)]
        a_t = [sb(f"a{r}", [128, TC], BF16) for r in range(4)]
        b_t = [sb(f"bt{r}", [128, TC], BF16) for r in range(2)]
        s_all = sb("s_all", [128, NST * TC], BF16)
        prod = sb("prod", [128, TC], BF16)
        yacc = sb("yacc", [128, TC], F32)
        xn1T = [sb(f"xn1T{m}", [128, TC], BF16) for m in range(8)]
        lg_sb = sb("lg_sb", [NE, TC], F32)
        lgt = [sb(f"lgt{t}", [128, NE], F32) for t in range(2)]
        gcol = sb("gcol", [128, 32], F32)
        msk = [sb(f"msk{t}", [128, NE], F32) for t in range(2)]
        eq1 = [sb(f"eq1{t}", [128, NE], F32) for t in range(2)]
        gts = [sb(f"gts{t}", [128, NE], F32) for t in range(2)]
        g_sb = sb("g_sb", [NE, TC], BF16)
        junk = sb("junk", [128, D], BF16)
        xn1Tf = [sb(f"xn1Tf{kk}", [128, TF], BF16) for kk in range(8)]
        g_f = sb("g_f", [NE, TF], BF16)
        gh = [sb(f"gh{fk}", [128, TF], BF16) for fk in range(8)]
        gtf = sb("gtf", [128, 32], F32)
        mixtmp = [sb(f"mxt{r}", [128, 512], BF16) for r in range(4)]
        w2c = [sb(f"w2c{s}", [128, 1024], BF16) for s in range(8)]
        NTB = TF // 128
        mixed = [sb(f"mix{tb}", [128, D], F32) for tb in range(NTB)]
        xn1pc = sb("xn1pc", [128, D], F32)
        x1pc = sb("x1pc", [128, D], F32)
        t_a = [sb(f"ta{r}", [128, D], F32) for r in range(2)]
        fcol = sb("fcol", [128, 8], F32)

        PB = [bb.ps(f"PB{i}", [128, 512], F32) for i in range(8)]

        T = bb.sem("T")
        V = bb.sem("V")
        S = bb.sem("S")
        G = bb.sem("G")

        sy = nc.sync
        te, ve, se, ge = nc.tensor, nc.vector, nc.scalar, nc.gpsimd

        def vf(ins):
            ev = V.inc(ins)
            bb.W(ve, ev)
            return ev

        def sf(ins):
            ev = S.inc(ins)
            bb.W(se, ev)
            return ev

        def dbg_dump(name, ap, wait):
            if not dbg or name not in dbg_outs:
                return
            bb.W(sy, wait)
            ins = sy.dma_start(out=dbg_outs[name], in_=ap)
            DS("dbg").inc(ins, 16)

        def dma(q, dsem, out, in_, wait=None):
            bb.W(q, wait)
            ins = q.dma_start(out=out, in_=in_)
            return dsem.inc(ins, 16)

        dsm = {}
        def DS(name):
            if name not in dsm:
                dsm[name] = bb.sem("d_" + name)
            return dsm[name]

        ring_free = [None] * NRING
        ring_ptr = [0]

        def ring_next():
            r = ring_ptr[0] % NRING
            ring_ptr[0] += 1
            return r

        def ring_load(src_ap):
            r = ring_next()
            ev = dma(sy, DS(f"ring{r}"), ring[r][:, :], src_ap, wait=ring_free[r])
            return r, ev

        tr_free = {0: None, 1: None}
        tr_ptr = [0]

        def next_tr():
            i = tr_ptr[0] % 2
            tr_ptr[0] += 1
            return i

        aux_free = {2: None, 3: None}
        aux_ptr = [0]

        def next_aux():
            i = 2 + (aux_ptr[0] % 2)
            aux_ptr[0] += 1
            return i

        for ci, cv in enumerate(cvals):
            G.inc(ge.memset(cst[:, ci:ci + 1], cv))
            nc.const_aps.register(F32, cv, cst[:, ci:ci + 1])
        G.inc(ge.memset(ones_bf[:, :], 1.0))
        G.inc(ge.memset(ones_f[:, :], 1.0))
        G.inc(ge.memset(s_state[:, :], 0.0))
        evm = G.inc(ge.memset(idf[:, :], 0.0))
        bb.W(ge, evm)
        ins = nc.gpsimd.affine_select(out=idf[:, :], in_=idf[:, :],
                                      compare_op=OP.not_equal, fill=1.0, base=0,
                                      pattern=[[-1, 128]], channel_multiplier=1)
        G.inc(ins)
        for i in range(16):
            G.inc(ge.memset(xin[i][:, 0:3], 0.0))
        ev_g0 = (G, G.n)

        ev_small = None
        for dst, src in [(A_sb[:, :], alog_h[:, :]), (cw_sb[:, :], cw_h[:, :]),
                         (cb_sb[:, :], cb_h[:, :]), (bdt_sb[:, :], bdt_h[:, :]),
                         (dp_sb[:, :], dpar_h[:, :]), (b1_sb[:, :], b1_h[:, :]),
                         (bg_c[:, :], bg_h[:, :]),
                         ]:
            ev_small = dma(sy, DS("small"), dst, src)
        bb.W(se, ev_g0)
        bb.W(ve, ev_g0)
        bb.W(se, ev_small)
        ins = se.activation(A_sb[:, :], A_sb[:, :], AF.Exp)
        ev_sA = S.inc(ins)
        bb.W(ve, ev_sA)
        ins = ve.tensor_scalar(A_sb[:, :], A_sb[:, :], -1.0, None, op0=OP.mult)
        ev_vA = V.inc(ins)

        row1_free = None
        for ri, (dst, srcrow) in enumerate([(g_bc, lng_h), (bb_bc, lnb_h),
                                            (gam_bc, gam_h), (bet_bc, bet_h)]):
            ev_row = dma(sy, DS("row1"), row1[0:1, :], srcrow[:, :], wait=row1_free)
            for h in range(2):
                pb = next_aux()
                bb.W(te, ev_g0)
                bb.W(te, aux_free[pb])
                bb.W(te, ev_row)
                ins = te.matmul(PB[pb][:, 0:512], ones_f[0:1, :],
                                row1[0:1, h * 512:(h + 1) * 512],
                                start=True, stop=True)
                ev_t = T.inc(ins)
                bb.W(se, ev_t)
                ins = se.activation(dst[:, h * 512:(h + 1) * 512], PB[pb][:, 0:512], AF.Copy)
                aux_free[pb] = S.inc(ins)
            row1_free = (T, T.n)
        ev_bcast = (S, S.n)

        ev_cast = ev_vA
        for kk in range(16):
            evl = dma(sy, DS("stg0"), t_a[0][:, 0:96], w_xp_h[kk * 128:(kk + 1) * 128, :],
                      wait=ev_cast)
            bb.W(ve, evl)
            ins = ve.tensor_copy(xprojb[:, kk * 96:(kk + 1) * 96], t_a[0][:, 0:96])
            ev_cast = V.inc(ins)
        evl = dma(sy, DS("stg0"), t_a[0][0:RK, 0:D], w_dt_h[:, 0:D], wait=ev_cast)
        evl2 = dma(sy, DS("stg1"), t_a[1][0:RK, 0:D], w_dt_h[:, D:DI])
        bb.W(ve, evl)
        ve.tensor_copy(wdtb[:, 0:D], t_a[0][0:RK, 0:D])
        bb.W(ve, evl2)
        ins = ve.tensor_copy(wdtb[:, D:DI], t_a[1][0:RK, 0:D])
        ev_cast = V.inc(ins)
        for kk in range(8):
            evl = dma(sy, DS("stg0"), t_a[0][:, 0:8], wg_h[kk * 128:(kk + 1) * 128, :], wait=ev_cast)
            bb.W(ve, evl)
            ins = ve.tensor_copy(wgb[:, kk * 8:(kk + 1) * 8], t_a[0][:, 0:8])
            ev_cast = V.inc(ins)
        evl = dma(sy, DS("stg0"), t_a[0][0:NE, 0:D], b2_h[:, :], wait=ev_cast)
        bb.W(ve, evl)
        ev_cast = V.inc(ve.tensor_copy(b2b[:, :], t_a[0][0:NE, 0:D]))

        stage_free = [ev_cast, ev_cast]
        q_tgl = [0]

        def prep_slab(src_f32_ap, out_maker):
            q = q_tgl[0]
            q_tgl[0] ^= 1
            evl = dma(nc.gpsimd, DS(f"pstg{q}"), t_a[q][:, :], src_f32_ap,
                      wait=stage_free[q])
            r = ring_next()
            bb.W(ve, ring_free[r])
            bb.W(ve, evl)
            ins = ve.tensor_copy(ring[r][:, :], t_a[q][:, :])
            evc = V.inc(ins)
            stage_free[q] = evc
            out_ap, in_ap = out_maker(ring[r])
            evo = dma(nc.gpsimd, DS(f"pring{r}"), out_ap, in_ap, wait=evc)
            ring_free[r] = evo
            return evo

        for kk in range(8):
            for q in range(4):
                def mk(slot, kk=kk, q=q):
                    o = winb_s[q * 8:(q + 1) * 8, :, kk * 128:(kk + 1) * 128]
                    o = o.rearrange("m p c -> p m c")
                    i = slot[:, :].rearrange("p (m c) -> p m c", m=8)
                    return o, i
                prep_slab(w_in_h[kk * 128:(kk + 1) * 128, q * 1024:(q + 1) * 1024], mk)
        for i in range(16):
            prep_slab(w_out_h[i * 128:(i + 1) * 128, :],
                      lambda slot, i=i: (woutb_s[i, :, :], slot[:, :]))
        ev_prep_all = [(dsm[f"pring{r}"], dsm[f"pring{r}"].n) for r in range(NRING)]

        piece_ev = {}
        slab_queue = []
        for e in range(NE):
            for kk in range(8):
                for h in range(2):
                    def mk(slot, e=e, kk=kk, h=h):
                        o = w1b_s[e * 16 + h * 8: e * 16 + (h + 1) * 8, :,
                                  kk * 128:(kk + 1) * 128]
                        o = o.rearrange("m p c -> p m c")
                        i = slot[:, :].rearrange("p (m c) -> p m c", m=8)
                        return o, i
                    keys = [("w1", e, h * 8 + f) for f in range(8)]
                    slab_queue.append((w1_h[e, kk * 128:(kk + 1) * 128,
                                            h * 1024:(h + 1) * 1024], mk, keys))
            for fk in range(16):
                slab_queue.append((w2_h[e, fk * 128:(fk + 1) * 128, :],
                                   lambda slot, e=e, fk=fk: (w2b_s[e * 16 + fk, :, :],
                                                             slot[:, :]),
                                   [("w2", e, fk)]))
        slab_pos = [0]

        def emit_slabs(nmax):
            n = 0
            while slab_pos[0] < len(slab_queue) and n < nmax:
                srcap, mkf, keys = slab_queue[slab_pos[0]]
                evo = prep_slab(srcap, mkf)
                for kkey in keys:
                    ev_old = piece_ev.get(kkey)
                    piece_ev[kkey] = [evo] if ev_old is None else ev_old + [evo]
                slab_pos[0] += 1
                n += 1

        xbuf_free = [[ev_cast], [ev_cast]]
        ybuf_free = [[ev_cast], [ev_cast]]
        a_slot_ev = [None] * 4
        spv_ev = [None, None]
        dls_ev = [None, None]
        gates_T_ev = None
        RT_evac_wait = None
        scan_done_prev = None
        BC_free = None
        Bbc_free = None
        psout_free = None
        xn1T_store_ev = None
        g_store_ev = None
        mstore_ev = [[] for _ in range(NCH)]
        wout_T_prev = None
        gt_store_prev = None
        bcrow_free = [None]
        fj = [0]

        fstate = dict(xn1Tf_free=None, gf_free=None, gh_free=None,
                      pcx_free=None, pcx1_free=None, mix_read_done=None,
                      last_out_ev=None, gtf_free=None)
        gbcf_free = [None, None]
        relu_free = [None, None]
        w2c_free = [None] * 8
        mixtmp_free = [None] * 4
        mxp = [0]
        mix_acc_evs = []
        fp_free = {6: None, 7: None}
        fp_ptr = [0]

        def next_fp():
            i2 = 6 + (fp_ptr[0] % 2)
            fp_ptr[0] += 1
            return i2

        fs = {"j": 0, "stage": 0, "evxT": None, "evg": None, "mix_init": None}

        def f_loads(j):
            t0 = j * TF
            xn1Tf_free = fstate["xn1Tf_free"]
            gf_free = fstate["gf_free"]
            dep = mstore_ev[2 * j] + mstore_ev[2 * j + 1]
            evxT = None
            for kk in range(8):
                evxT = dma(sy, DS("xf"), xn1Tf[kk][:, :],
                           xn1T_s[kk * 128:(kk + 1) * 128, t0:t0 + TF],
                           wait=dep + [xn1Tf_free] if kk == 0 else None)
            evg = dma(sy, DS("gf"), g_f[:, :], g_s[:, t0:t0 + TF], wait=gf_free)
            evgt = None
            for tb in range(NTB):
                evgt = dma(sy, DS("gtf"), gtf[:, tb * 8:(tb + 1) * 8],
                           gt_s[t0 + tb * 128: t0 + (tb + 1) * 128, :],
                           wait=fstate["gtf_free"] if tb == 0 else None)
            fs["evxT"], fs["evg"], fs["evgt_s"] = evxT, evg, evgt
            mix_acc_evs.clear()
            mix_read_done = fstate["mix_read_done"]
            for grp in range(NTB * 2):
                tb, dh = grp // 2, grp % 2
                pbm = next_fp()
                bb.W(te, fp_free[pbm])
                bb.W(te, evg)
                ins = te.matmul(PB[pbm][:, 0:512],
                                g_f[:, tb * 128:(tb + 1) * 128],
                                b2b[:, dh * 512:(dh + 1) * 512],
                                start=True, stop=True)
                ev_t = T.inc(ins)
                dst = mixed[tb][:, dh * 512:(dh + 1) * 512]
                sl = mxp[0] % 4
                mxp[0] += 1
                mt = mixtmp[sl]
                bb.W(se, ev_t)
                bb.W(se, mixtmp_free[sl])
                ins = se.activation(mt[:, :], PB[pbm][:, 0:512], AF.Copy)
                ev_s = S.inc(ins)
                fp_free[pbm] = ev_s
                ev_d = dma(ge, DS(f"mxt{sl}"), dst, mt[:, :],
                           wait=[ev_s, mix_read_done])
                mixtmp_free[sl] = ev_d
                mix_acc_evs.append(ev_d)

        def f_w1(j, e, half):
            evxT = fs["evxT"]
            gh_free = fstate["gh_free"]
            evw2 = []
            for s8 in range(8):
                fk = half * 8 + s8
                ev = dma(sy, DS(f"w2c{s8}"), w2c[s8][:, :], w2b_s[e * 16 + fk, :, :],
                         wait=[w2c_free[s8]] + piece_ev[("w2", e, fk)])
                evw2.append(ev)
            fs["evw2"] = evw2
            ev_gh_all = None
            for s8 in range(8):
                fk = half * 8 + s8
                bb.W(sy, piece_ev[("w1", e, fk)])
                r, evw = ring_load(w1b_s[e * 16 + fk, :, :])
                pbh = next_fp()
                bb.W(te, fp_free[pbh])
                bb.W(te, evw)
                bb.W(te, evxT)
                for kk in range(8):
                    ins = te.matmul(PB[pbh][:, 0:TF],
                                    ring[r][:, kk * 128:(kk + 1) * 128],
                                    xn1Tf[kk][:, :],
                                    start=(kk == 0), stop=(kk == 7))
                ev_t = T.inc(ins)
                ring_free[r] = ev_t
                bb.W(se, ev_t)
                bb.W(se, gh_free)
                ins = se.activation(gh[s8][:, :], PB[pbh][:, 0:TF], AF.Relu,
                                    bias=b1_sb[:, e * 16 + fk: e * 16 + fk + 1])
                ev_rl = S.inc(ins)
                fp_free[pbh] = ev_rl
            fs["ev_gh_all"] = (S, S.n)

        def f_w2(j, e, half):
            t0 = j * TF
            ev_gh_all = fs["ev_gh_all"]
            evw2 = fs["evw2"]
            mix_read_done = fstate["mix_read_done"]
            for grp in range(NTB * 2):
                tb, dh = grp // 2, grp % 2
                pbm = next_fp()
                bb.W(te, fp_free[pbm])
                bb.W(te, ev_gh_all)
                bb.W(te, evw2)
                for s8 in range(8):
                    ins = te.matmul(PB[pbm][:, 0:512],
                                    gh[s8][:, tb * 128:(tb + 1) * 128],
                                    w2c[s8][:, dh * 512:(dh + 1) * 512],
                                    start=(s8 == 0), stop=(s8 == 7))
                ev_t = T.inc(ins)
                dst = mixed[tb][:, dh * 512:(dh + 1) * 512]
                sl = mxp[0] % 4
                mxp[0] += 1
                mt = mixtmp[sl]
                bb.W(se, ev_t)
                bb.W(se, mixtmp_free[sl])
                bb.W(se, fs["evgt_s"])
                ins = se.activation(mt[:, :], PB[pbm][:, 0:512], AF.Identity,
                                    scale=gtf[:, tb * 8 + e: tb * 8 + e + 1])
                ev_s = S.inc(ins)
                fp_free[pbm] = ev_s
                bb.W(ge, ev_s)
                ins = ge.dma_start(out=dst, in_=mt[:, :], accum_op=OP.add)
                ev_d = DS(f"mxt{sl}").inc(ins, 16)
                mixtmp_free[sl] = ev_d
                mix_acc_evs.append(ev_d)
            fstate["gh_free"] = (T, T.n)
            for s8 in range(8):
                w2c_free[s8] = (T, T.n)
            if e == NE - 1 and half == 1:
                fstate["xn1Tf_free"] = (T, T.n)
                fstate["gf_free"] = (T, T.n)
                fstate["gtf_free"] = (V, V.n)

        def f_out(j):
            t0 = j * TF
            pcx_free = fstate["pcx_free"]
            pcx1_free = fstate["pcx1_free"]
            ev_mix_all = [(dsm[f"mxt{sl2}"], dsm[f"mxt{sl2}"].n) for sl2 in range(4)]
            if j == 0:
                dbg_dump("mix0", mixed[0][:, :], ev_mix_all)
            for tb in range(NTB):
                tt0 = t0 + tb * 128
                evx = dma(sy, DS("pcx"), xn1pc[:, :], xn1_s[tt0:tt0 + 128, :], wait=pcx_free)
                evx1 = dma(sy, DS("pcx1"), x1pc[:, :], x1_s[tt0:tt0 + 128, :], wait=pcx1_free)
                c = lambda jj: fcol[:, jj:jj + 1]
                bb.W(ve, ev_mix_all)
                bb.W(ve, evx)
                ins = ve.tensor_tensor(mixed[tb][:, :], mixed[tb][:, :], xn1pc[:, :], op=OP.add)
                ev_y = V.inc(ins)
                ev_red = V.inc(ve.reduce_sum(c(0), mixed[tb][:, :], axis=AX.X))
                bb.W(se, ev_y)
                se.activation(junk[:, :], mixed[tb][:, :], AF.Square, accum_out=c(1))
                ins = se.activation(c(1), c(1), AF.Copy)
                ev_sq = S.inc(ins)
                bb.W(ve, ev_red)
                vf(ve.tensor_scalar(c(2), c(0), 1.0 / D, None, op0=OP.mult))
                bb.W(ve, ev_sq)
                vf(ve.tensor_scalar(c(3), c(1), 1.0 / D, None, op0=OP.mult))
                vf(ve.tensor_tensor(c(4), c(2), c(2), op=OP.mult))
                ins = ve.tensor_tensor(c(3), c(3), c(4), op=OP.subtract)
                ev_var = V.inc(ins)
                bb.W(se, ev_var)
                sf(se.activation(c(4), c(3), AF.Ln, bias=EPS))
                ins = se.activation(c(4), c(4), AF.Exp, scale=-0.5)
                ev_r4 = S.inc(ins)
                ta = t_a[tb % 2]
                bb.W(ve, ev_r4)
                bb.W(ve, stage_free[tb % 2] if j == 0 else None)
                ve.tensor_scalar(ta[:, :], mixed[tb][:, :], c(2), c(4),
                                 op0=OP.subtract, op1=OP.mult)
                ve.tensor_tensor(ta[:, :], ta[:, :], gam_bc[:, :], op=OP.mult)
                ve.tensor_tensor(ta[:, :], ta[:, :], bet_bc[:, :], op=OP.add)
                bb.W(ve, evx1)
                ins = ve.tensor_tensor(xn1pc[:, :], ta[:, :], x1pc[:, :], op=OP.add)
                ev_out = V.inc(ins)
                pcx1_free = ev_out
                fstate["pcx1_free"] = ev_out
                ev_st = dma(sy, DS("out"), out_h[tt0:tt0 + 128, :], xn1pc[:, :], wait=ev_out)
                pcx_free = ev_st
                fstate["pcx_free"] = ev_st
                fstate["last_out_ev"] = ev_st
            fstate["mix_read_done"] = (V, V.n)

        NSTG = NE * 4 + 2
        TOT_STAGES = NFC * NSTG

        def fpump(k_done, budget=1, pace=True, frac=0.0):
            while budget > 0:
                j, st = fs["j"], fs["stage"]
                if j >= NFC or 2 * j + 1 > k_done:
                    return
                done = j * NSTG + st
                if pace and k_done < NCH - 1:
                    target = TOT_STAGES * (k_done + frac - 1.0) / (NCH - 3.0)
                    if done >= target:
                        return
                if st == 0:
                    need = 32
                    if slab_pos[0] < need * NE:
                        pass
                    f_loads(j)
                elif st <= NE * 4:
                    q2, part = divmod(st - 1, 2)
                    e, half = divmod(q2, 2)
                    if slab_pos[0] < 32 * (e + 1):
                        emit_slabs(32 * (e + 1) - slab_pos[0])
                    if part == 0:
                        f_w1(j, e, half)
                    else:
                        f_w2(j, e, half)
                else:
                    f_out(j)
                    fs["j"] += 1
                    fs["stage"] = 0
                    budget -= 1
                    continue
                fs["stage"] += 1
                budget -= 1

        for k in range(NCH):
            t0 = k * TC
            par = k % 2
            emit_slabs(24)
            xev = []
            for tt in range(2):
                ev = dma(sy, DS(f"xb{tt}"), xbuf[tt][:, :],
                         x_h[t0 + tt * 128: t0 + (tt + 1) * 128, :],
                         wait=xbuf_free[tt])
                xev.append(ev)

            if k == 0:
                dbg_dump("xb0", xbuf[0][:, :], xev[0])
            ev_h0 = None
            for tt in range(2):
                c = lambda j: scol[:, par * 16 + tt * 8 + j: par * 16 + tt * 8 + j + 1]
                bb.W(ve, xev[tt])
                ev_red = V.inc(ve.reduce_sum(c(0), xbuf[tt][:, :], axis=AX.X))
                bb.W(se, xev[tt])
                se.activation(junk[:, :], xbuf[tt][:, :], AF.Square, accum_out=c(1))
                ins = se.activation(c(1), c(1), AF.Copy)
                ev_sq = S.inc(ins)
                bb.W(ve, ev_red)
                vf(ve.tensor_scalar(c(2), c(0), 1.0 / D, None, op0=OP.mult))
                bb.W(ve, ev_sq)
                vf(ve.tensor_scalar(c(3), c(1), 1.0 / D, None, op0=OP.mult))
                vf(ve.tensor_tensor(c(4), c(2), c(2), op=OP.mult))
                ins = ve.tensor_tensor(c(3), c(3), c(4), op=OP.subtract)
                ev_var = V.inc(ins)
                bb.W(se, ev_var)
                sf(se.activation(c(4), c(3), AF.Ln, bias=EPS))
                sf(se.activation(c(4), c(4), AF.Exp, scale=-0.5))
                sf(se.activation(c(5), c(3), AF.Ln, scale=1.0 + EPS, bias=EPS * EPS))
                ins = se.activation(c(5), c(5), AF.Exp, scale=-0.5)
                ev_r = S.inc(ins)
                bb.W(ve, ev_r)
                bb.W(ve, ybuf_free[tt])
                ve.tensor_scalar(ybuf[tt][:, :], xbuf[tt][:, :], c(2), c(5),
                                 op0=OP.subtract, op1=OP.mult)
                bb.W(ve, ev_bcast)
                ve.tensor_tensor(ybuf[tt][:, :], ybuf[tt][:, :], g_bc[:, :], op=OP.mult)
                ve.tensor_tensor(ybuf[tt][:, :], ybuf[tt][:, :], bb_bc[:, :], op=OP.add)
                vf(ve.tensor_scalar(c(6), c(4), 1.0, None, op0=OP.add))
                vf(ve.tensor_tensor(c(7), c(2), c(4), op=OP.mult))
                ins = ve.tensor_scalar(xbuf[tt][:, :], xbuf[tt][:, :], c(6), c(7),
                                        op0=OP.mult, op1=OP.subtract)
                ev_h0 = V.inc(ins)
            ev_ln = ev_h0

            tr_jobs = ([("R", tt, dc) for tt in range(2) for dc in range(8)]
                       + [("H", tt, dc) for tt in range(2) for dc in range(8)])
            ev_evacR = None
            ev_evacH = None
            for g0 in range(0, 32, 4):
                pb = next_tr()
                bb.W(te, tr_free[pb])
                bb.W(te, ev_ln)
                bb.W(te, ev_g0)
                for jj in range(4):
                    kind, tt, dc = tr_jobs[g0 + jj]
                    src = xbuf[tt] if kind == "R" else ybuf[tt]
                    ins = te.transpose(PB[pb][:, jj * 128:(jj + 1) * 128],
                                       src[:, dc * 128:(dc + 1) * 128], idf[:, :])
                ev_t = T.inc(ins)
                bb.W(se, ev_t)
                bb.W(se, RT_evac_wait)
                for jj in range(4):
                    kind, tt, dc = tr_jobs[g0 + jj]
                    dst = (RT[dc] if kind == "R" else h0T[dc])[:, tt * 128:(tt + 1) * 128]
                    ins = se.activation(dst, PB[pb][:, jj * 128:(jj + 1) * 128], AF.Copy)
                ev_s = S.inc(ins)
                tr_free[pb] = ev_s
                if g0 < 16:
                    ev_evacR = ev_s
                ev_evacH = ev_s
            ev_R_T_done = None

            xin_ev = [None] * 16
            for m in range(32):
                r, evw = ring_load(winb_s[m, :, :])
                if k == 0:
                    bb.W(sy, ev_prep_all)
                pb = next_aux()
                bb.W(te, aux_free[pb])
                bb.W(te, evw)
                bb.W(te, ev_evacH)
                for kk in range(8):
                    ins = te.matmul(PB[pb][:, 0:TC], ring[r][:, kk * 128:(kk + 1) * 128],
                                    h0T[kk][:, :], start=(kk == 0), stop=(kk == 7))
                ev_t = T.inc(ins)
                ring_free[r] = ev_t
                bb.W(se, ev_t)
                bb.W(se, scan_done_prev)
                bb.W(se, wout_T_prev)
                if m < 16:
                    ins = se.activation(xin[m][:, 3:3 + TC], PB[pb][:, 0:TC], AF.Copy)
                else:
                    ins = se.activation(zsil[m - 16][:, :], PB[pb][:, 0:TC], AF.Silu)
                ev_s = S.inc(ins)
                aux_free[pb] = ev_s
                if m < 16:
                    xin_ev[m] = ev_s
            ev_xin = (S, S.n)
            fpump(k - 1, frac=0.25, budget=2)
            if k == 0:
                dbg_dump("sc0", scol[:, 0:16], ev_ln)
                dbg_dump("yb0", ybuf[0][:, :], ev_ln)
                dbg_dump("h0T0", h0T[0][:, :], ev_evacH)
                dbg_dump("RT0", RT[0][:, :], ev_evacR)
                dbg_dump("zsil0", zsil[0][:, :], ev_xin)
                dbg_dump("A0", A_sb[:, :], ev_vA)

            ev_xc = None
            silu_ev = [None, None]
            for i in range(16):
                ca = cacc[i % 2]
                bb.W(ve, xin_ev[i])
                bb.W(ve, silu_ev[i % 2])
                ve.tensor_scalar(ca[:, :], xin[i][:, 0:TC], cw_sb[:, 4 * i:4 * i + 1],
                                 None, op0=OP.mult)
                for j in range(1, 4):
                    ins = ve.scalar_tensor_tensor(ca[:, :], in0=xin[i][:, j:j + TC],
                                                  scalar=cw_sb[:, 4 * i + j:4 * i + j + 1],
                                                  in1=ca[:, :], op0=OP.mult, op1=OP.add)
                ev_cv = V.inc(ins)
                bb.W(ge, ev_cv)
                ins = ge.tensor_copy(xin[i][:, 0:3], xin[i][:, TC:TC + 3])
                ev_gt = G.inc(ins)
                bb.W(se, ev_cv)
                bb.W(se, ev_gt)
                ins = se.activation(xin[i][:, 3:3 + TC], ca[:, :], AF.Silu,
                                    bias=cb_sb[:, i:i + 1])
                ev_xc = S.inc(ins)
                silu_ev[i % 2] = ev_xc
            ev_xc_all = ev_xc
            fpump(k - 1, frac=0.35, budget=2)
            if k == 0:
                dbg_dump("xin0", xin[0][:, 0:TC + 3], ev_xc_all)
                dbg_dump("xpb", xprojb[:, :], ev_cast)
                if dbg:
                    for i2 in range(16):
                        bb.W(sy, ev_xc_all)
                        ins = sy.dma_start(out=dbg_outs["xcall"][:, i2 * TC:(i2 + 1) * TC],
                                           in_=xin[i2][:, 3:3 + TC])
                        DS("dbg").inc(ins, 16)

            pb_dbc = next_aux()
            bb.W(te, aux_free[pb_dbc])
            bb.W(te, ev_xc_all)
            bb.W(te, ev_cast)
            for kk in range(16):
                ins = te.matmul(PB[pb_dbc][0:96, 0:TC], xprojb[:, kk * 96:(kk + 1) * 96],
                                xin[kk][:, 3:3 + TC], start=(kk == 0), stop=(kk == 15))
            ev_dbc = T.inc(ins)
            bb.W(se, ev_dbc)
            ins = se.activation(dt_b[:, :], PB[pb_dbc][0:RK, 0:TC], AF.Copy)
            ev_dt = S.inc(ins)
            bb.W(ve, ev_dbc)
            bb.W(ve, ev_dt)
            bb.W(ve, BC_free)
            ins = ve.tensor_copy(BC_sb[64:96, :], PB[pb_dbc][64:96, 0:TC])
            ev_bc_ev = V.inc(ins)
            aux_free[pb_dbc] = ev_bc_ev
            bcn_ev = [None] * 16
            ccn_ev = [None] * 16
            for cc in range(2):
                ev_bcrow = dma(sy, DS("bcrow"), BCrow[0:1, :],
                               BC_sb[64 + cc * 16:80 + cc * 16, :],
                               wait=[ev_bc_ev, bcrow_free[0]])
                for n in range(16):
                    pb = next_aux()
                    bb.W(te, aux_free[pb])
                    bb.W(te, ev_bcrow)
                    ins = te.matmul(PB[pb][:, 0:TC], ones_bf[0:1, :],
                                    BCrow[0:1, n * TC:(n + 1) * TC], start=True, stop=True)
                    ev_t = T.inc(ins)
                    dst = B_bc[n] if cc == 0 else C_bc[n]
                    bb.W(se, ev_t)
                    bb.W(se, scan_done_prev)
                    ins = se.activation(dst[:, :], PB[pb][:, 0:TC], AF.Copy)
                    ev_s = S.inc(ins)
                    aux_free[pb] = ev_s
                    (bcn_ev if cc == 0 else ccn_ev)[n] = ev_s
                bcrow_free[0] = (T, T.n)
            BC_done = (S, S.n)
            BC_free = ev_bcrow
            fpump(k - 1, frac=0.45, budget=2)
            if k == 0:
                dbg_dump("Bbc0", B_bc[0][:, :], BC_done)
                dbg_dump("Cbc0", C_bc[0][:, :], BC_done)
                dbg_dump("dtb0", dt_b[:, :], ev_dt)

            dl_s = [None, None]
            ev_dl_p = [None, None]

            def delta_chain(i):
                pb = next_aux()
                bb.W(te, aux_free[pb])
                bb.W(te, ev_dt)
                bb.W(te, ev_cast)
                ins = te.matmul(PB[pb][:, 0:TC], wdtb[:, i * 128:(i + 1) * 128],
                                dt_b[:, :], start=True, stop=True)
                ev_t = T.inc(ins)
                sp = sp_t[i % 2]
                bb.W(se, ev_t)
                bb.W(se, spv_ev[i % 2])
                se.activation(sp[:, :], PB[pb][:, 0:TC], AF.Abs, bias=bdt_sb[:, i:i + 1])
                se.activation(sp[:, :], sp[:, :], AF.Exp, scale=-1.0)
                ins = se.activation(sp[:, :], sp[:, :], AF.Ln, bias=1.0)
                ev_sp = S.inc(ins)
                dl = dlt[i % 2]
                bb.W(ve, ev_sp)
                bb.W(ve, dls_ev[i % 2])
                ve.scalar_tensor_tensor(dl[:, :], in0=PB[pb][:, 0:TC],
                                        scalar=bdt_sb[:, i:i + 1], in1=sp[:, :],
                                        op0=OP.add, op1=OP.add)
                ins = ve.tensor_tensor(dl[:, :], dl[:, :], sp[:, :], op=OP.max)
                ev_dl = V.inc(ins)
                aux_free[pb] = ev_dl
                spv_ev[i % 2] = ev_dl
                d_u = du[i % 2]
                ins = ve.tensor_tensor(d_u[:, :], dl[:, :], xin[i][:, 3:3 + TC], op=OP.mult)
                V.inc(ins)
                ev_dl_p[i % 2] = ev_dl
                if k == 0 and i == 0:
                    dbg_dump("dlt0", dl[:, :], ev_dl)

            delta_chain(0)
            for i in range(16):
                dl = dlt[i % 2]
                d_u = du[i % 2]
                bb.W(se, ev_dl_p[i % 2])
                ev_scan = None
                scan_ev11 = None
                for n in range(NST):
                    sl = a_t[n % 4]
                    bb.W(se, a_slot_ev[n % 4])
                    ins = se.activation(sl[:, :], dl[:, :], AF.Exp,
                                        scale=A_sb[:, i * 16 + n: i * 16 + n + 1])
                    a_ev = S.inc(ins)
                    bbuf = b_t[n % 2]
                    ss = s_all[:, n * TC:(n + 1) * TC]
                    bb.W(ve, bcn_ev[n])
                    ve.tensor_tensor(bbuf[:, :], d_u[:, :], B_bc[n][:, :], op=OP.mult)
                    bb.W(ve, a_ev)
                    ins = ve.tensor_tensor_scan(
                        ss, sl[:, :], bbuf[:, :],
                        initial=s_state[:, i * 16 + n: i * 16 + n + 1],
                        op0=OP.mult, op1=OP.add)
                    ev_scan = V.inc(ins)
                    a_slot_ev[n % 4] = ev_scan
                dls_ev[i % 2] = (S, S.n)
                if i + 1 < 16:
                    delta_chain(i + 1)
                bb.W(ve, ev_scan)
                ve.tensor_copy(s_state[:, i * 16:(i + 1) * 16],
                               s_all[:, TC - 1::TC])
                bb.W(ve, ccn_ev[15])
                for n in range(NST):
                    ve.tensor_tensor(s_all[:, n * TC:(n + 1) * TC],
                                     s_all[:, n * TC:(n + 1) * TC],
                                     C_bc[n][:, :], op=OP.mult)
                for step in (1, 2, 4, 8):
                    for base in range(0, NST, 2 * step):
                        ve.tensor_tensor(s_all[:, base * TC:(base + 1) * TC],
                                         s_all[:, base * TC:(base + 1) * TC],
                                         s_all[:, (base + step) * TC:(base + step + 1) * TC],
                                         op=OP.add)
                ve.scalar_tensor_tensor(yacc[:, :], in0=xin[i][:, 3:3 + TC],
                                        scalar=dp_sb[:, i:i + 1],
                                        in1=s_all[:, 0:TC],
                                        op0=OP.mult, op1=OP.add)
                ins = ve.tensor_tensor(xin[i][:, 3:3 + TC], yacc[:, :],
                                       zsil[i][:, :], op=OP.mult)
                ev_yg = V.inc(ins)
                if k == 0 and i == 0:
                    dbg_dump("yg0", xin[i][:, 3:3 + TC], ev_yg)
                fpump(k - 1, frac=0.45 + (i + 1) * 0.032, budget=2)
            scan_done_prev = (V, V.n)
            ev_yg_all = scan_done_prev

            bb.W(ve, ev_evacR)
            for dh in range(2):
                for i in range(16):
                    r, evw = ring_load(woutb_s[i, :, :])
                    bb.W(te, evw)
                    bb.W(te, ev_yg_all)
                    if i == 0:
                        bb.W(te, psout_free)
                    for m in range(4):
                        mm = dh * 4 + m
                        pbo = 4 + m // 2
                        ins = te.matmul(PB[pbo][:, (m % 2) * TC:(m % 2 + 1) * TC],
                                        ring[r][:, mm * 128:(mm + 1) * 128],
                                        xin[i][:, 3:3 + TC],
                                        start=(i == 0 and m % 2 == 0),
                                        stop=(i == 15 and m % 2 == 1))
                    ev_wout_last = T.inc(ins)
                    ring_free[r] = ev_wout_last
                bb.W(ve, ev_wout_last)
                for m in range(4):
                    mm = dh * 4 + m
                    pbo = 4 + m // 2
                    ins = ve.tensor_tensor(RT[mm][:, :], RT[mm][:, :],
                                           PB[pbo][:, (m % 2) * TC:(m % 2 + 1) * TC],
                                           op=OP.add)
                psout_free = V.inc(ins)
            ev_x1T = psout_free
            wout_T_prev = (T, T.n)
            fpump(k - 1, frac=0.97, budget=2)

            ev_x1tok = None
            for g0 in range(0, 16, 4):
                pb = next_tr()
                bb.W(te, tr_free[pb])
                bb.W(te, ev_x1T)
                for jj in range(4):
                    idx = g0 + jj
                    tt, dc = idx // 8, idx % 8
                    ins = te.transpose(PB[pb][:, jj * 128:(jj + 1) * 128],
                                       RT[dc][:, tt * 128:(tt + 1) * 128], idf[:, :])
                ev_t = T.inc(ins)
                bb.W(se, ev_t)
                for jj in range(4):
                    idx = g0 + jj
                    tt, dc = idx // 8, idx % 8
                    ins = se.activation(xbuf[tt][:, dc * 128:(dc + 1) * 128],
                                        PB[pb][:, jj * 128:(jj + 1) * 128], AF.Copy)
                ev_s = S.inc(ins)
                tr_free[pb] = ev_s
                ev_x1tok = ev_s
            RT_evac_wait = (T, T.n)

            ev_xn1 = None
            for tt in range(2):
                c = lambda j: scol[:, par * 16 + tt * 8 + j: par * 16 + tt * 8 + j + 1]
                bb.W(ve, ev_x1tok)
                ev_red = V.inc(ve.reduce_sum(c(0), xbuf[tt][:, :], axis=AX.X))
                bb.W(se, ev_x1tok)
                se.activation(junk[:, :], xbuf[tt][:, :], AF.Square, accum_out=c(1))
                ins = se.activation(c(1), c(1), AF.Copy)
                ev_sq = S.inc(ins)
                bb.W(ve, ev_red)
                vf(ve.tensor_scalar(c(2), c(0), 1.0 / D, None, op0=OP.mult))
                bb.W(ve, ev_sq)
                vf(ve.tensor_scalar(c(3), c(1), 1.0 / D, None, op0=OP.mult))
                vf(ve.tensor_tensor(c(4), c(2), c(2), op=OP.mult))
                ins = ve.tensor_tensor(c(3), c(3), c(4), op=OP.subtract)
                ev_var = V.inc(ins)
                bb.W(se, ev_var)
                sf(se.activation(c(4), c(3), AF.Ln, bias=EPS))
                ins = se.activation(c(4), c(4), AF.Exp, scale=-0.5)
                ev_r3 = S.inc(ins)
                bb.W(ve, ev_r3)
                bb.W(ve, ybuf_free[tt])
                ins = ve.tensor_scalar(ybuf[tt][:, :], xbuf[tt][:, :], c(2), c(4),
                                        op0=OP.subtract, op1=OP.mult)
                ev_xn1 = V.inc(ins)
                ev_st1 = dma(sy, DS(f"xb{tt}"), x1_s[t0 + tt * 128: t0 + (tt + 1) * 128, :],
                             xbuf[tt][:, :], wait=ev_x1tok)
                ev_st2 = dma(sy, DS(f"yb{tt}"), xn1_s[t0 + tt * 128: t0 + (tt + 1) * 128, :],
                             ybuf[tt][:, :], wait=ev_xn1)
                xbuf_free[tt] = [ev_st1]
            ev_xn1_all = ev_xn1

            ev_xn1T = None
            for g0 in range(0, 16, 4):
                pb = next_tr()
                bb.W(te, tr_free[pb])
                bb.W(te, ev_xn1_all)
                for jj in range(4):
                    idx = g0 + jj
                    tt, dc = idx // 8, idx % 8
                    ins = te.transpose(PB[pb][:, jj * 128:(jj + 1) * 128],
                                       ybuf[tt][:, dc * 128:(dc + 1) * 128], idf[:, :])
                ev_t = T.inc(ins)
                bb.W(se, ev_t)
                bb.W(se, xn1T_store_ev)
                for jj in range(4):
                    idx = g0 + jj
                    tt, dc = idx // 8, idx % 8
                    ins = se.activation(xn1T[dc][:, tt * 128:(tt + 1) * 128],
                                        PB[pb][:, jj * 128:(jj + 1) * 128], AF.Copy)
                ev_s = S.inc(ins)
                tr_free[pb] = ev_s
                ev_xn1T = ev_s
            for tt in range(2):
                ybuf_free[tt] = [(T, T.n), (dsm[f"yb{tt}"], dsm[f"yb{tt}"].n)]
            ev_xn1T_st = None
            for dc in range(8):
                ev_xn1T_st = dma(sy, DS("xt"), xn1T_s[dc * 128:(dc + 1) * 128, t0:t0 + TC],
                                 xn1T[dc][:, :], wait=ev_xn1T)
            xn1T_store_ev = ev_xn1T_st

            pb = next_aux()
            bb.W(te, aux_free[pb])
            bb.W(te, ev_xn1T)
            for kk in range(8):
                ins = te.matmul(PB[pb][0:NE, 0:TC], wgb[:, kk * 8:(kk + 1) * 8],
                                xn1T[kk][:, :], start=(kk == 0), stop=(kk == 7))
            ev_t = T.inc(ins)
            bb.W(se, ev_t)
            ins = se.activation(lg_sb[:, :], PB[pb][0:NE, 0:TC], AF.Identity,
                                bias=bg_c[:, :])
            ev_lg = S.inc(ins)
            aux_free[pb] = ev_lg
            pbt = next_aux()
            bb.W(te, aux_free[pbt])
            bb.W(te, ev_lg)
            for tt in range(2):
                ins = te.transpose(PB[pbt][0:128, tt * 8:(tt + 1) * 8],
                                   lg_sb[:, tt * 128:(tt + 1) * 128], idf[0:NE, 0:NE])
            ev_t = T.inc(ins)
            bb.W(ve, ev_t)
            for tt in range(2):
                vf(ve.tensor_copy(lgt[tt][:, :], PB[pbt][0:128, tt * 8:(tt + 1) * 8]))
            ev_lgt = (V, V.n)
            aux_free[pbt] = ev_lgt
            ev_gts = None
            bb.W(ve, gates_T_ev)
            bb.W(ve, gt_store_prev)
            for tt in range(2):
                c = lambda j: gcol[:, par * 16 + tt * 8 + j: par * 16 + tt * 8 + j + 1]
                ev_rm1 = V.inc(ve.reduce_max(c(0), lgt[tt][:, :], axis=AX.X))
                bb.W(ve, ev_rm1)
                vf(ve.tensor_scalar(eq1[tt][:, :], lgt[tt][:, :], c(0), None, op0=OP.is_ge))
                vf(ve.scalar_tensor_tensor(msk[tt][:, :], in0=eq1[tt][:, :], scalar=-1e30,
                                           in1=lgt[tt][:, :], op0=OP.mult, op1=OP.add))
                ev_rm2 = V.inc(ve.reduce_max(c(1), msk[tt][:, :], axis=AX.X))
                bb.W(ve, ev_rm2)
                ins = ve.tensor_tensor(c(2), c(1), c(0), op=OP.subtract)
                ev_d12 = V.inc(ins)
                bb.W(se, ev_d12)
                ins = se.activation(c(3), c(2), AF.Exp)
                ev_e = S.inc(ins)
                bb.W(ve, ev_e)
                vf(ve.tensor_scalar(c(4), c(3), 1.0, None, op0=OP.add))
                vf(ve.reciprocal(c(4), c(4)))
                vf(ve.tensor_scalar(c(5), c(4), -1.0, 1.0, op0=OP.mult, op1=OP.add))
                vf(ve.tensor_scalar(msk[tt][:, :], msk[tt][:, :], c(1), None, op0=OP.is_ge))
                vf(ve.tensor_scalar(gts[tt][:, :], eq1[tt][:, :], c(4), None, op0=OP.mult))
                ins = ve.scalar_tensor_tensor(gts[tt][:, :], in0=msk[tt][:, :], scalar=c(5),
                                              in1=gts[tt][:, :], op0=OP.mult, op1=OP.add)
                ev_gts = V.inc(ins)
            pbt = next_aux()
            bb.W(te, aux_free[pbt])
            bb.W(te, ev_gts)
            for tt in range(2):
                ins = te.transpose(PB[pbt][0:NE, tt * 128:(tt + 1) * 128],
                                   gts[tt][:, :], idf[:, :])
            ev_t = T.inc(ins)
            gates_T_ev = ev_t
            bb.W(se, ev_t)
            bb.W(se, g_store_ev)
            for tt in range(2):
                ins = se.activation(g_sb[:, tt * 128:(tt + 1) * 128],
                                    PB[pbt][0:NE, tt * 128:(tt + 1) * 128], AF.Copy)
            ev_gsb = S.inc(ins)
            aux_free[pbt] = ev_gsb
            g_store_ev = dma(sy, DS("g"), g_s[:, t0:t0 + TC], g_sb[:, :], wait=ev_gsb)
            ev_gt_st = None
            for tt in range(2):
                ev_gt_st = dma(sy, DS("gt"), gt_s[t0 + tt * 128: t0 + (tt + 1) * 128, :],
                               gts[tt][:, :], wait=ev_gts if tt == 1 else None)
            gt_store_prev = ev_gt_st
            mstore_ev[k] += [xbuf_free[0][0], xbuf_free[1][0],
                             (dsm["yb0"], dsm["yb0"].n), (dsm["yb1"], dsm["yb1"].n),
                             (dsm["xt"], dsm["xt"].n), g_store_ev, ev_gt_st]
            fpump(k, budget=2)


        while fs["j"] < NFC:
            fpump(NCH - 1, budget=100, pace=False)

        bb.W(sy, fstate["last_out_ev"])
        bb.W(sy, (dsm["out"], dsm["out"].n))

    return nc


_NC_CACHE = {}


def _get_nc():
    if "nc" not in _NC_CACHE:
        _NC_CACHE["nc"] = build_nc()
    return _NC_CACHE["nc"]


def kernel(**inputs):
    g = lambda n: np.asarray(inputs[n])
    x = g("L0_attn_x")
    f32 = np.float32

    common = {
        "w_in": np.ascontiguousarray(g("L0_attn_W_in"), dtype=f32),
        "w_out": np.ascontiguousarray(g("L0_attn_W_out"), dtype=f32),
        "w_xp": np.ascontiguousarray(g("L0_attn_W_xproj"), dtype=f32),
        "w_dt": np.ascontiguousarray(g("L0_attn_W_dt"), dtype=f32),
        "w1": np.ascontiguousarray(g("L0_ffn_W1"), dtype=f32),
        "w2": np.ascontiguousarray(g("L0_ffn_W2"), dtype=f32),
        "wg": np.ascontiguousarray(g("L0_ffn_Wg"), dtype=f32),
        "lng": np.ascontiguousarray(g("L0_attn_ln_g").reshape(1, D), dtype=f32),
        "lnb": np.ascontiguousarray(g("L0_attn_ln_b").reshape(1, D), dtype=f32),
        "gam": np.ascontiguousarray(g("L0_ffn_gamma").reshape(1, D), dtype=f32),
        "bet": np.ascontiguousarray(g("L0_ffn_beta").reshape(1, D), dtype=f32),
        "alog": np.ascontiguousarray(
            g("L0_attn_A_log").reshape(16, 128, NST).transpose(1, 0, 2).reshape(128, 16 * NST),
            dtype=f32),
        "cw": np.ascontiguousarray(
            g("L0_attn_conv_w").reshape(DI, 4).reshape(16, 128, 4)
            .transpose(1, 0, 2).reshape(128, 64), dtype=f32),
        "cb": np.ascontiguousarray(g("L0_attn_conv_b").reshape(16, 128).T, dtype=f32),
        "bdt": np.ascontiguousarray(g("L0_attn_b_dt").reshape(16, 128).T, dtype=f32),
        "dpar": np.ascontiguousarray(g("L0_attn_D_param").reshape(16, 128).T, dtype=f32),
        "b1": np.ascontiguousarray(
            g("L0_ffn_b1").reshape(NE, 16, 128).transpose(2, 0, 1).reshape(128, 128),
            dtype=f32),
        "b2": np.ascontiguousarray(g("L0_ffn_b2"), dtype=f32),
        "bg": np.ascontiguousarray(g("L0_ffn_bg").reshape(NE, 1), dtype=f32),
    }
    in_maps = [dict(common, x=np.ascontiguousarray(x[b], dtype=f32)) for b in range(8)]
    nc = _get_nc()
    res = run_bass_kernel_spmd(nc, in_maps, core_ids=list(range(8)))
    out = np.stack([np.asarray(r["out"]) for r in res.results], axis=0)
    return out.astype(np.float32)


if __name__ == "__main__":
    nc = build_nc()
    print("built ok")


# version 15
# speedup vs baseline: 1.1170x; 1.0519x over previous
import contextlib
import numpy as np

import concourse.bass as bass
import concourse.mybir as mybir
from concourse.alu_op_type import AluOpType as OP
from concourse.bass_utils import run_bass_kernel_spmd

F32 = mybir.dt.float32
BF16 = mybir.dt.bfloat16
AF = mybir.ActivationFunctionType
AX = mybir.AxisListType

L = 4096
D = 1024
DI = 2048
NST = 16
RK = 64
NE = 8
FF = 2048
TC = 256
TF = 512
NCH = L // TC
NFC = L // TF
EPS = 1e-5


class Sem:
    def __init__(self, es, nc, name):
        self.h = es.enter_context(nc.semaphore(name))
        self.n = 0

    def inc(self, ins, k=1):
        self.n += k
        ins.then_inc(self.h, k)
        return (self, self.n)

    @property
    def ev(self):
        return (self, self.n)


class B:

    def __init__(self, nc, es):
        self.nc = nc
        self.es = es
        self._waited = {}

    def sem(self, name):
        return Sem(self.es, self.nc, name)

    def sb(self, name, shape, dt):
        return self.es.enter_context(self.nc.sbuf_tensor(name, shape, dt))

    def ps(self, name, shape, dt):
        return self.es.enter_context(self.nc.psum_tensor(name, shape, dt))

    def W(self, eng, ev):
        if ev is None:
            return
        if isinstance(ev, list):
            for e in ev:
                self.W(eng, e)
            return
        sem, n = ev
        if n <= 0:
            return
        assert n <= sem.n, f"wait target {n} > emitted {sem.n}: deadlock"
        key = (id(eng), id(sem))
        if self._waited.get(key, 0) >= n:
            return
        eng.wait_ge(sem.h, n)
        self._waited[key] = n


def build_nc(dbg=False):
    nc = bass.Bass("TRN2", target_bir_lowering=False,
                   detect_race_conditions=False)
    es = contextlib.ExitStack()
    bb = B(nc, es)

    dr = lambda n, s, t, k: nc.dram_tensor(n, s, t, kind=k).ap()
    x_h = dr("x", [L, D], F32, "ExternalInput")
    out_h = dr("out", [L, D], F32, "ExternalOutput")
    w_in_h = dr("w_in", [D, 2 * DI], F32, "ExternalInput")
    w_out_h = dr("w_out", [DI, D], F32, "ExternalInput")
    w_xp_h = dr("w_xp", [DI, 96], F32, "ExternalInput")
    w_dt_h = dr("w_dt", [RK, DI], F32, "ExternalInput")
    w1_h = dr("w1", [NE, D, FF], F32, "ExternalInput")
    w2_h = dr("w2", [NE, FF, D], F32, "ExternalInput")
    wg_h = dr("wg", [D, NE], F32, "ExternalInput")
    lng_h = dr("lng", [1, D], F32, "ExternalInput")
    lnb_h = dr("lnb", [1, D], F32, "ExternalInput")
    gam_h = dr("gam", [1, D], F32, "ExternalInput")
    bet_h = dr("bet", [1, D], F32, "ExternalInput")
    alog_h = dr("alog", [128, 16 * NST], F32, "ExternalInput")
    cw_h = dr("cw", [128, 16 * 4], F32, "ExternalInput")
    cb_h = dr("cb", [128, 16], F32, "ExternalInput")
    bdt_h = dr("bdt", [128, 16], F32, "ExternalInput")
    dpar_h = dr("dpar", [128, 16], F32, "ExternalInput")
    b1_h = dr("b1", [128, 128], F32, "ExternalInput")
    b2_h = dr("b2", [NE, D], F32, "ExternalInput")
    bg_h = dr("bg", [NE, 1], F32, "ExternalInput")

    sk = "ExternalOutput" if dbg else "Internal"
    dbg_outs = {}
    def dbg_decl(name, shape, dt=F32):
        if dbg:
            dbg_outs[name] = nc.dram_tensor("dbg_" + name, shape, dt, kind="ExternalOutput").ap()
    winb_s = dr("winb_s", [32, 128, 1024], BF16, "Internal")
    woutb_s = dr("woutb_s", [16, 128, 1024], BF16, "Internal")
    w1b_s = dr("w1b_s", [NE * 16, 128, 1024], BF16, "Internal")
    w2b_s = dr("w2b_s", [NE * 16, 128, 1024], BF16, "Internal")
    x1_s = dr("x1_s", [L, D], F32, sk)
    xn1_s = dr("xn1_s", [L, D], F32, sk)
    xn1T_s = dr("xn1T_s", [D, L], BF16, "Internal")
    g_s = dr("g_s", [NE, L], BF16, sk)
    gt_s = dr("gt_s", [L, NE], F32, "Internal")

    dbg_decl("h0T0", [128, TC], BF16); dbg_decl("RT0", [128, TC])
    dbg_decl("xin0", [128, TC + 3], BF16); dbg_decl("zsil0", [128, TC], BF16)
    dbg_decl("dlt0", [128, TC]); dbg_decl("du0", [128, TC], BF16)
    dbg_decl("a0", [128, TC], BF16); dbg_decl("Bbc0", [128, TC], BF16)
    dbg_decl("Cbc0", [128, TC], BF16); dbg_decl("s0", [128, TC], BF16)
    dbg_decl("yacc0", [128, TC]); dbg_decl("yg0", [128, TC], BF16)
    dbg_decl("A0", [128, 16 * NST]); dbg_decl("dtb0", [RK, TC], BF16)
    dbg_decl("xb0", [128, D]); dbg_decl("sc0", [128, 16]); dbg_decl("yb0", [128, D])
    dbg_decl("xpb", [128, 16 * 96], BF16); dbg_decl("xcall", [128, 16 * TC], BF16)
    dbg_decl("mix0", [128, D]); dbg_decl("gh0", [128, TF], BF16); dbg_decl("gbcf0", [128, TF], BF16)
    with es:
        sb = bb.sb
        ones_bf = sb("ones_bf", [1, 128], BF16)
        ones_f = sb("ones_f", [1, 128], F32)
        idf = sb("idf", [128, 128], F32)
        g_bc = sb("g_bc", [128, D], BF16)
        bb_bc = sb("bb_bc", [128, D], BF16)
        gam_bc = sb("gam_bc", [128, D], BF16)
        bet_bc = sb("bet_bc", [128, D], BF16)
        xprojb = sb("xprojb", [128, 16 * 96], BF16)
        wdtb = sb("wdtb", [RK, DI], BF16)
        wgb = sb("wgb", [128, NE * 8], BF16)
        b2b = sb("b2b", [NE, D], BF16)
        bg_c = sb("bg_c", [NE, 1], F32)
        A_sb = sb("A_sb", [128, 16 * NST], F32)
        cw_sb = sb("cw_sb", [128, 64], F32)
        cb_sb = sb("cb_sb", [128, 16], F32)
        bdt_sb = sb("bdt_sb", [128, 16], F32)
        dp_sb = sb("dp_sb", [128, 16], F32)
        b1_sb = sb("b1_sb", [128, 128], F32)
        row1 = sb("row1", [1, D], F32)
        s_state = sb("s_state", [128, 16 * NST], F32)
        cvals = [EPS, EPS * EPS, 1.0 + EPS, 1.0, -1.0, -0.5, 0.0]
        cst = sb("cst", [128, len(cvals)], F32)
        NRING = 7
        ring = [sb(f"ring{r}", [128, 1024], BF16) for r in range(NRING)]
        xbuf = [sb(f"xbuf{t}", [128, D], F32) for t in range(2)]
        ybuf = [sb(f"ybuf{t}", [128, D], F32) for t in range(2)]
        scol = sb("scol", [128, 32], F32)
        RT = [sb(f"RT{m}", [128, TC], F32) for m in range(8)]
        h0T = [sb(f"h0T{m}", [128, TC], BF16) for m in range(8)]
        xin = [sb(f"xin{i}", [128, TC + 4], BF16) for i in range(16)]
        cacc = [sb(f"cacc{r}", [128, TC], F32) for r in range(2)]
        zsil = [sb(f"zsil{i}", [128, TC], BF16) for i in range(16)]
        dt_b = sb("dt_b", [RK, TC], BF16)
        BC_sb = sb("BC_sb", [128, TC], BF16)
        BCrow = sb("BCrow", [1, 16 * TC], BF16)
        B_bc = [sb(f"Bbc{n}", [128, TC], BF16) for n in range(NST)]
        C_bc = [sb(f"Cbc{n}", [128, TC], BF16) for n in range(NST)]
        dlt = [sb(f"dlt{r}", [128, TC], F32) for r in range(2)]
        du = [sb(f"du{r}", [128, TC], BF16) for r in range(2)]
        sp_t = [sb(f"spt{r}", [128, TC], F32) for r in range(2)]
        a_t = [sb(f"a{r}", [128, TC], BF16) for r in range(4)]
        b_t = [sb(f"bt{r}", [128, TC], BF16) for r in range(2)]
        s_all = sb("s_all", [128, NST * TC], BF16)
        prod = sb("prod", [128, TC], BF16)
        yacc = sb("yacc", [128, TC], F32)
        xn1T = [sb(f"xn1T{m}", [128, TC], BF16) for m in range(8)]
        lg_sb = sb("lg_sb", [NE, TC], F32)
        lgt = [sb(f"lgt{t}", [128, NE], F32) for t in range(2)]
        gcol = sb("gcol", [128, 32], F32)
        msk = [sb(f"msk{t}", [128, NE], F32) for t in range(2)]
        eq1 = [sb(f"eq1{t}", [128, NE], F32) for t in range(2)]
        gts = [sb(f"gts{t}", [128, NE], F32) for t in range(2)]
        g_sb = sb("g_sb", [NE, TC], BF16)
        junk = sb("junk", [128, D], BF16)
        xn1Tf = [sb(f"xn1Tf{kk}", [128, TF], BF16) for kk in range(8)]
        g_f = sb("g_f", [NE, TF], BF16)
        gh = [sb(f"gh{fk}", [128, TF], BF16) for fk in range(8)]
        gtf = sb("gtf", [128, 32], F32)
        mixtmp = [sb(f"mxt{r}", [128, 512], BF16) for r in range(4)]
        w2c = [sb(f"w2c{s}", [128, 1024], BF16) for s in range(8)]
        NTB = TF // 128
        mixed = [sb(f"mix{tb}", [128, D], F32) for tb in range(NTB)]
        xn1pc = sb("xn1pc", [128, D], F32)
        x1pc = sb("x1pc", [128, D], F32)
        t_a = [sb(f"ta{r}", [128, D], F32) for r in range(2)]
        fcol = sb("fcol", [128, 8], F32)

        PB = [bb.ps(f"PB{i}", [128, 512], F32) for i in range(8)]

        T = bb.sem("T")
        V = bb.sem("V")
        S = bb.sem("S")
        G = bb.sem("G")

        sy = nc.sync
        te, ve, se, ge = nc.tensor, nc.vector, nc.scalar, nc.gpsimd

        def vf(ins):
            ev = V.inc(ins)
            bb.W(ve, ev)
            return ev

        def sf(ins):
            ev = S.inc(ins)
            bb.W(se, ev)
            return ev

        def dbg_dump(name, ap, wait):
            if not dbg or name not in dbg_outs:
                return
            bb.W(sy, wait)
            ins = sy.dma_start(out=dbg_outs[name], in_=ap)
            DS("dbg").inc(ins, 16)

        def dma(q, dsem, out, in_, wait=None):
            bb.W(q, wait)
            ins = q.dma_start(out=out, in_=in_)
            return dsem.inc(ins, 16)

        dsm = {}
        def DS(name):
            if name not in dsm:
                dsm[name] = bb.sem("d_" + name)
            return dsm[name]

        ring_free = [None] * NRING
        ring_ptr = [0]

        def ring_next():
            r = ring_ptr[0] % NRING
            ring_ptr[0] += 1
            return r

        def ring_load(src_ap):
            r = ring_next()
            ev = dma(sy, DS(f"ring{r}"), ring[r][:, :], src_ap, wait=ring_free[r])
            return r, ev

        tr_free = {0: None, 1: None}
        tr_ptr = [0]

        def next_tr():
            i = tr_ptr[0] % 2
            tr_ptr[0] += 1
            return i

        aux_free = {2: None, 3: None}
        aux_ptr = [0]

        def next_aux():
            i = 2 + (aux_ptr[0] % 2)
            aux_ptr[0] += 1
            return i

        for ci, cv in enumerate(cvals):
            G.inc(ge.memset(cst[:, ci:ci + 1], cv))
            nc.const_aps.register(F32, cv, cst[:, ci:ci + 1])
        G.inc(ge.memset(ones_bf[:, :], 1.0))
        G.inc(ge.memset(ones_f[:, :], 1.0))
        G.inc(ge.memset(s_state[:, :], 0.0))
        evm = G.inc(ge.memset(idf[:, :], 0.0))
        bb.W(ge, evm)
        ins = nc.gpsimd.affine_select(out=idf[:, :], in_=idf[:, :],
                                      compare_op=OP.not_equal, fill=1.0, base=0,
                                      pattern=[[-1, 128]], channel_multiplier=1)
        G.inc(ins)
        for i in range(16):
            G.inc(ge.memset(xin[i][:, 0:3], 0.0))
        ev_g0 = (G, G.n)

        ev_small = None
        for dst, src in [(A_sb[:, :], alog_h[:, :]), (cw_sb[:, :], cw_h[:, :]),
                         (cb_sb[:, :], cb_h[:, :]), (bdt_sb[:, :], bdt_h[:, :]),
                         (dp_sb[:, :], dpar_h[:, :]), (b1_sb[:, :], b1_h[:, :]),
                         (bg_c[:, :], bg_h[:, :]),
                         ]:
            ev_small = dma(sy, DS("small"), dst, src)
        bb.W(se, ev_g0)
        bb.W(ve, ev_g0)
        bb.W(se, ev_small)
        ins = se.activation(A_sb[:, :], A_sb[:, :], AF.Exp)
        ev_sA = S.inc(ins)
        bb.W(ve, ev_sA)
        ins = ve.tensor_scalar(A_sb[:, :], A_sb[:, :], -1.0, None, op0=OP.mult)
        ev_vA = V.inc(ins)

        row1_free = None
        for ri, (dst, srcrow) in enumerate([(g_bc, lng_h), (bb_bc, lnb_h),
                                            (gam_bc, gam_h), (bet_bc, bet_h)]):
            ev_row = dma(sy, DS("row1"), row1[0:1, :], srcrow[:, :], wait=row1_free)
            for h in range(2):
                pb = next_aux()
                bb.W(te, ev_g0)
                bb.W(te, aux_free[pb])
                bb.W(te, ev_row)
                ins = te.matmul(PB[pb][:, 0:512], ones_f[0:1, :],
                                row1[0:1, h * 512:(h + 1) * 512],
                                start=True, stop=True)
                ev_t = T.inc(ins)
                bb.W(se, ev_t)
                ins = se.activation(dst[:, h * 512:(h + 1) * 512], PB[pb][:, 0:512], AF.Copy)
                aux_free[pb] = S.inc(ins)
            row1_free = (T, T.n)
        ev_bcast = (S, S.n)

        ev_cast = ev_vA
        for kk in range(16):
            evl = dma(sy, DS("stg0"), t_a[0][:, 0:96], w_xp_h[kk * 128:(kk + 1) * 128, :],
                      wait=ev_cast)
            bb.W(ve, evl)
            ins = ve.tensor_copy(xprojb[:, kk * 96:(kk + 1) * 96], t_a[0][:, 0:96])
            ev_cast = V.inc(ins)
        evl = dma(sy, DS("stg0"), t_a[0][0:RK, 0:D], w_dt_h[:, 0:D], wait=ev_cast)
        evl2 = dma(sy, DS("stg1"), t_a[1][0:RK, 0:D], w_dt_h[:, D:DI])
        bb.W(ve, evl)
        ve.tensor_copy(wdtb[:, 0:D], t_a[0][0:RK, 0:D])
        bb.W(ve, evl2)
        ins = ve.tensor_copy(wdtb[:, D:DI], t_a[1][0:RK, 0:D])
        ev_cast = V.inc(ins)
        for kk in range(8):
            evl = dma(sy, DS("stg0"), t_a[0][:, 0:8], wg_h[kk * 128:(kk + 1) * 128, :], wait=ev_cast)
            bb.W(ve, evl)
            ins = ve.tensor_copy(wgb[:, kk * 8:(kk + 1) * 8], t_a[0][:, 0:8])
            ev_cast = V.inc(ins)
        evl = dma(sy, DS("stg0"), t_a[0][0:NE, 0:D], b2_h[:, :], wait=ev_cast)
        bb.W(ve, evl)
        ev_cast = V.inc(ve.tensor_copy(b2b[:, :], t_a[0][0:NE, 0:D]))

        stage_free = [ev_cast, ev_cast]
        q_tgl = [0]

        def prep_slab(src_f32_ap, out_maker):
            q = q_tgl[0]
            q_tgl[0] ^= 1
            evl = dma(nc.gpsimd, DS(f"pstg{q}"), t_a[q][:, :], src_f32_ap,
                      wait=stage_free[q])
            r = ring_next()
            bb.W(ve, ring_free[r])
            bb.W(ve, evl)
            ins = ve.tensor_copy(ring[r][:, :], t_a[q][:, :])
            evc = V.inc(ins)
            stage_free[q] = evc
            out_ap, in_ap = out_maker(ring[r])
            evo = dma(nc.gpsimd, DS(f"pring{r}"), out_ap, in_ap, wait=evc)
            ring_free[r] = evo
            return evo

        for kk in range(8):
            for q in range(4):
                def mk(slot, kk=kk, q=q):
                    o = winb_s[q * 8:(q + 1) * 8, :, kk * 128:(kk + 1) * 128]
                    o = o.rearrange("m p c -> p m c")
                    i = slot[:, :].rearrange("p (m c) -> p m c", m=8)
                    return o, i
                prep_slab(w_in_h[kk * 128:(kk + 1) * 128, q * 1024:(q + 1) * 1024], mk)
        for i in range(16):
            prep_slab(w_out_h[i * 128:(i + 1) * 128, :],
                      lambda slot, i=i: (woutb_s[i, :, :], slot[:, :]))
        ev_prep_all = [(dsm[f"pring{r}"], dsm[f"pring{r}"].n) for r in range(NRING)]

        piece_ev = {}
        slab_queue = []
        for e in range(NE):
            for kk in range(8):
                for h in range(2):
                    def mk(slot, e=e, kk=kk, h=h):
                        o = w1b_s[e * 16 + h * 8: e * 16 + (h + 1) * 8, :,
                                  kk * 128:(kk + 1) * 128]
                        o = o.rearrange("m p c -> p m c")
                        i = slot[:, :].rearrange("p (m c) -> p m c", m=8)
                        return o, i
                    keys = [("w1", e, h * 8 + f) for f in range(8)]
                    slab_queue.append((w1_h[e, kk * 128:(kk + 1) * 128,
                                            h * 1024:(h + 1) * 1024], mk, keys))
            for fk in range(16):
                slab_queue.append((w2_h[e, fk * 128:(fk + 1) * 128, :],
                                   lambda slot, e=e, fk=fk: (w2b_s[e * 16 + fk, :, :],
                                                             slot[:, :]),
                                   [("w2", e, fk)]))
        slab_pos = [0]

        def emit_slabs(nmax):
            n = 0
            while slab_pos[0] < len(slab_queue) and n < nmax:
                srcap, mkf, keys = slab_queue[slab_pos[0]]
                evo = prep_slab(srcap, mkf)
                for kkey in keys:
                    ev_old = piece_ev.get(kkey)
                    piece_ev[kkey] = [evo] if ev_old is None else ev_old + [evo]
                slab_pos[0] += 1
                n += 1

        xbuf_free = [[ev_cast], [ev_cast]]
        ybuf_free = [[ev_cast], [ev_cast]]
        a_slot_ev = [None] * 4
        spv_ev = [None, None]
        dls_ev = [None, None]
        gates_T_ev = None
        RT_evac_wait = None
        scan_done_prev = None
        BC_free = None
        Bbc_free = None
        psout_free = None
        xn1T_store_ev = None
        g_store_ev = None
        mstore_ev = [[] for _ in range(NCH)]
        wout_T_prev = None
        gt_store_prev = None
        bcrow_free = [None]
        fj = [0]

        fstate = dict(xn1Tf_free=None, gf_free=None, gh_free=None,
                      pcx_free=None, pcx1_free=None, mix_read_done=None,
                      last_out_ev=None, gtf_free=None)
        gbcf_free = [None, None]
        relu_free = [None, None]
        w2c_free = [None] * 8
        mixtmp_free = [None] * 4
        mxp = [0]
        mix_acc_evs = []
        fp_free = {6: None, 7: None}
        fp_ptr = [0]

        def next_fp():
            i2 = 6 + (fp_ptr[0] % 2)
            fp_ptr[0] += 1
            return i2

        fs = {"j": 0, "stage": 0, "evxT": None, "evg": None, "mix_init": None}

        def f_loads(j):
            t0 = j * TF
            xn1Tf_free = fstate["xn1Tf_free"]
            gf_free = fstate["gf_free"]
            dep = mstore_ev[2 * j] + mstore_ev[2 * j + 1]
            evxT = None
            for kk in range(8):
                evxT = dma(sy, DS("xf"), xn1Tf[kk][:, :],
                           xn1T_s[kk * 128:(kk + 1) * 128, t0:t0 + TF],
                           wait=dep + [xn1Tf_free] if kk == 0 else None)
            evg = dma(sy, DS("gf"), g_f[:, :], g_s[:, t0:t0 + TF], wait=gf_free)
            evgt = None
            for tb in range(NTB):
                evgt = dma(sy, DS("gtf"), gtf[:, tb * 8:(tb + 1) * 8],
                           gt_s[t0 + tb * 128: t0 + (tb + 1) * 128, :],
                           wait=fstate["gtf_free"] if tb == 0 else None)
            fs["evxT"], fs["evg"], fs["evgt_s"] = evxT, evg, evgt
            mix_acc_evs.clear()
            mix_read_done = fstate["mix_read_done"]
            for grp in range(NTB * 2):
                tb, dh = grp // 2, grp % 2
                pbm = next_fp()
                bb.W(te, fp_free[pbm])
                bb.W(te, evg)
                ins = te.matmul(PB[pbm][:, 0:512],
                                g_f[:, tb * 128:(tb + 1) * 128],
                                b2b[:, dh * 512:(dh + 1) * 512],
                                start=True, stop=True)
                ev_t = T.inc(ins)
                dst = mixed[tb][:, dh * 512:(dh + 1) * 512]
                sl = mxp[0] % 4
                mxp[0] += 1
                mt = mixtmp[sl]
                bb.W(se, ev_t)
                bb.W(se, mixtmp_free[sl])
                ins = se.activation(mt[:, :], PB[pbm][:, 0:512], AF.Copy)
                ev_s = S.inc(ins)
                fp_free[pbm] = ev_s
                ev_d = dma(ge, DS(f"mxt{sl}"), dst, mt[:, :],
                           wait=[ev_s, mix_read_done])
                mixtmp_free[sl] = ev_d
                mix_acc_evs.append(ev_d)

        def f_w1(j, e, half):
            evxT = fs["evxT"]
            gh_free = fstate["gh_free"]
            evw2 = []
            for s8 in range(8):
                fk = half * 8 + s8
                ev = dma(sy, DS(f"w2c{s8}"), w2c[s8][:, :], w2b_s[e * 16 + fk, :, :],
                         wait=[w2c_free[s8]] + piece_ev[("w2", e, fk)])
                evw2.append(ev)
            fs["evw2"] = evw2
            ev_gh_all = None
            for s8 in range(8):
                fk = half * 8 + s8
                bb.W(sy, piece_ev[("w1", e, fk)])
                r, evw = ring_load(w1b_s[e * 16 + fk, :, :])
                pbh = next_fp()
                bb.W(te, fp_free[pbh])
                bb.W(te, evw)
                bb.W(te, evxT)
                for kk in range(8):
                    ins = te.matmul(PB[pbh][:, 0:TF],
                                    ring[r][:, kk * 128:(kk + 1) * 128],
                                    xn1Tf[kk][:, :],
                                    start=(kk == 0), stop=(kk == 7))
                ev_t = T.inc(ins)
                ring_free[r] = ev_t
                bb.W(se, ev_t)
                bb.W(se, gh_free)
                ins = se.activation(gh[s8][:, :], PB[pbh][:, 0:TF], AF.Relu,
                                    bias=b1_sb[:, e * 16 + fk: e * 16 + fk + 1])
                ev_rl = S.inc(ins)
                fp_free[pbh] = ev_rl
            fs["ev_gh_all"] = (S, S.n)

        def f_w2(j, e, half):
            t0 = j * TF
            ev_gh_all = fs["ev_gh_all"]
            evw2 = fs["evw2"]
            mix_read_done = fstate["mix_read_done"]
            for grp in range(NTB * 2):
                tb, dh = grp // 2, grp % 2
                pbm = next_fp()
                bb.W(te, fp_free[pbm])
                bb.W(te, ev_gh_all)
                bb.W(te, evw2)
                for s8 in range(8):
                    ins = te.matmul(PB[pbm][:, 0:512],
                                    gh[s8][:, tb * 128:(tb + 1) * 128],
                                    w2c[s8][:, dh * 512:(dh + 1) * 512],
                                    start=(s8 == 0), stop=(s8 == 7))
                ev_t = T.inc(ins)
                dst = mixed[tb][:, dh * 512:(dh + 1) * 512]
                sl = mxp[0] % 4
                mxp[0] += 1
                mt = mixtmp[sl]
                bb.W(se, ev_t)
                bb.W(se, mixtmp_free[sl])
                bb.W(se, fs["evgt_s"])
                ins = se.activation(mt[:, :], PB[pbm][:, 0:512], AF.Identity,
                                    scale=gtf[:, tb * 8 + e: tb * 8 + e + 1])
                ev_s = S.inc(ins)
                fp_free[pbm] = ev_s
                bb.W(ge, ev_s)
                ins = ge.dma_start(out=dst, in_=mt[:, :], accum_op=OP.add)
                ev_d = DS(f"mxt{sl}").inc(ins, 16)
                mixtmp_free[sl] = ev_d
                mix_acc_evs.append(ev_d)
            fstate["gh_free"] = (T, T.n)
            for s8 in range(8):
                w2c_free[s8] = (T, T.n)
            if e == NE - 1 and half == 1:
                fstate["xn1Tf_free"] = (T, T.n)
                fstate["gf_free"] = (T, T.n)
                fstate["gtf_free"] = (V, V.n)

        def f_out(j):
            t0 = j * TF
            pcx_free = fstate["pcx_free"]
            pcx1_free = fstate["pcx1_free"]
            ev_mix_all = [(dsm[f"mxt{sl2}"], dsm[f"mxt{sl2}"].n) for sl2 in range(4)]
            if j == 0:
                dbg_dump("mix0", mixed[0][:, :], ev_mix_all)
            for tb in range(NTB):
                tt0 = t0 + tb * 128
                evx = dma(sy, DS("pcx"), xn1pc[:, :], xn1_s[tt0:tt0 + 128, :], wait=pcx_free)
                evx1 = dma(sy, DS("pcx1"), x1pc[:, :], x1_s[tt0:tt0 + 128, :], wait=pcx1_free)
                c = lambda jj: fcol[:, jj:jj + 1]
                bb.W(ve, ev_mix_all)
                bb.W(ve, evx)
                ins = ve.tensor_tensor(mixed[tb][:, :], mixed[tb][:, :], xn1pc[:, :], op=OP.add)
                ev_y = V.inc(ins)
                ev_red = V.inc(ve.reduce_sum(c(0), mixed[tb][:, :], axis=AX.X))
                bb.W(se, ev_y)
                se.activation(junk[:, :], mixed[tb][:, :], AF.Square, accum_out=c(1))
                ins = se.activation(c(1), c(1), AF.Copy)
                ev_sq = S.inc(ins)
                bb.W(ve, ev_red)
                vf(ve.tensor_scalar(c(2), c(0), 1.0 / D, None, op0=OP.mult))
                bb.W(ve, ev_sq)
                vf(ve.tensor_scalar(c(3), c(1), 1.0 / D, None, op0=OP.mult))
                vf(ve.tensor_tensor(c(4), c(2), c(2), op=OP.mult))
                ins = ve.tensor_tensor(c(3), c(3), c(4), op=OP.subtract)
                ev_var = V.inc(ins)
                bb.W(se, ev_var)
                sf(se.activation(c(4), c(3), AF.Ln, bias=EPS))
                ins = se.activation(c(4), c(4), AF.Exp, scale=-0.5)
                ev_r4 = S.inc(ins)
                ta = t_a[tb % 2]
                bb.W(ve, ev_r4)
                bb.W(ve, stage_free[tb % 2] if j == 0 else None)
                ve.tensor_scalar(ta[:, :], mixed[tb][:, :], c(2), c(4),
                                 op0=OP.subtract, op1=OP.mult)
                ve.tensor_tensor(ta[:, :], ta[:, :], gam_bc[:, :], op=OP.mult)
                ve.tensor_tensor(ta[:, :], ta[:, :], bet_bc[:, :], op=OP.add)
                bb.W(ve, evx1)
                ins = ve.tensor_tensor(xn1pc[:, :], ta[:, :], x1pc[:, :], op=OP.add)
                ev_out = V.inc(ins)
                pcx1_free = ev_out
                fstate["pcx1_free"] = ev_out
                ev_st = dma(sy, DS("out"), out_h[tt0:tt0 + 128, :], xn1pc[:, :], wait=ev_out)
                pcx_free = ev_st
                fstate["pcx_free"] = ev_st
                fstate["last_out_ev"] = ev_st
            fstate["mix_read_done"] = (V, V.n)

        NSTG = NE * 4 + 2
        TOT_STAGES = NFC * NSTG

        def fpump(k_done, budget=1, pace=True, frac=0.0):
            while budget > 0:
                j, st = fs["j"], fs["stage"]
                if j >= NFC or 2 * j + 1 > k_done:
                    return
                done = j * NSTG + st
                if pace and k_done < NCH - 1:
                    target = TOT_STAGES * (k_done + frac - 1.0) / (NCH - 3.0)
                    if done >= target:
                        return
                if st == 0:
                    need = 32
                    if slab_pos[0] < need * NE:
                        pass
                    f_loads(j)
                elif st <= NE * 4:
                    q2, part = divmod(st - 1, 2)
                    e, half = divmod(q2, 2)
                    if slab_pos[0] < 32 * (e + 1):
                        emit_slabs(32 * (e + 1) - slab_pos[0])
                    if part == 0:
                        f_w1(j, e, half)
                    else:
                        f_w2(j, e, half)
                else:
                    f_out(j)
                    fs["j"] += 1
                    fs["stage"] = 0
                    budget -= 1
                    continue
                fs["stage"] += 1
                budget -= 1

        for k in range(NCH):
            t0 = k * TC
            par = k % 2
            emit_slabs(24)
            xev = []
            for tt in range(2):
                ev = dma(sy, DS(f"xb{tt}"), xbuf[tt][:, :],
                         x_h[t0 + tt * 128: t0 + (tt + 1) * 128, :],
                         wait=xbuf_free[tt])
                xev.append(ev)

            if k == 0:
                dbg_dump("xb0", xbuf[0][:, :], xev[0])
            ev_h0 = None
            for tt in range(2):
                c = lambda j: scol[:, par * 16 + tt * 8 + j: par * 16 + tt * 8 + j + 1]
                bb.W(ve, xev[tt])
                ev_red = V.inc(ve.reduce_sum(c(0), xbuf[tt][:, :], axis=AX.X))
                bb.W(se, xev[tt])
                se.activation(junk[:, :], xbuf[tt][:, :], AF.Square, accum_out=c(1))
                ins = se.activation(c(1), c(1), AF.Copy)
                ev_sq = S.inc(ins)
                bb.W(ve, ev_red)
                vf(ve.tensor_scalar(c(2), c(0), 1.0 / D, None, op0=OP.mult))
                bb.W(ve, ev_sq)
                vf(ve.tensor_scalar(c(3), c(1), 1.0 / D, None, op0=OP.mult))
                vf(ve.tensor_tensor(c(4), c(2), c(2), op=OP.mult))
                ins = ve.tensor_tensor(c(3), c(3), c(4), op=OP.subtract)
                ev_var = V.inc(ins)
                bb.W(se, ev_var)
                sf(se.activation(c(4), c(3), AF.Ln, bias=EPS))
                sf(se.activation(c(4), c(4), AF.Exp, scale=-0.5))
                sf(se.activation(c(5), c(3), AF.Ln, scale=1.0 + EPS, bias=EPS * EPS))
                ins = se.activation(c(5), c(5), AF.Exp, scale=-0.5)
                ev_r = S.inc(ins)
                bb.W(ve, ev_r)
                bb.W(ve, ybuf_free[tt])
                ve.tensor_scalar(ybuf[tt][:, :], xbuf[tt][:, :], c(2), c(5),
                                 op0=OP.subtract, op1=OP.mult)
                bb.W(ve, ev_bcast)
                ve.tensor_tensor(ybuf[tt][:, :], ybuf[tt][:, :], g_bc[:, :], op=OP.mult)
                ve.tensor_tensor(ybuf[tt][:, :], ybuf[tt][:, :], bb_bc[:, :], op=OP.add)
                vf(ve.tensor_scalar(c(6), c(4), 1.0, None, op0=OP.add))
                vf(ve.tensor_tensor(c(7), c(2), c(4), op=OP.mult))
                ins = ve.tensor_scalar(xbuf[tt][:, :], xbuf[tt][:, :], c(6), c(7),
                                        op0=OP.mult, op1=OP.subtract)
                ev_h0 = V.inc(ins)
            ev_ln = ev_h0

            tr_jobs = ([("R", tt, dc) for tt in range(2) for dc in range(8)]
                       + [("H", tt, dc) for tt in range(2) for dc in range(8)])
            ev_evacR = None
            ev_evacH = None
            for g0 in range(0, 32, 4):
                pb = next_tr()
                bb.W(te, tr_free[pb])
                bb.W(te, ev_ln)
                bb.W(te, ev_g0)
                for jj in range(4):
                    kind, tt, dc = tr_jobs[g0 + jj]
                    src = xbuf[tt] if kind == "R" else ybuf[tt]
                    ins = te.transpose(PB[pb][:, jj * 128:(jj + 1) * 128],
                                       src[:, dc * 128:(dc + 1) * 128], idf[:, :])
                ev_t = T.inc(ins)
                bb.W(se, ev_t)
                bb.W(se, RT_evac_wait)
                for jj in range(4):
                    kind, tt, dc = tr_jobs[g0 + jj]
                    dst = (RT[dc] if kind == "R" else h0T[dc])[:, tt * 128:(tt + 1) * 128]
                    ins = se.activation(dst, PB[pb][:, jj * 128:(jj + 1) * 128], AF.Copy)
                ev_s = S.inc(ins)
                tr_free[pb] = ev_s
                if g0 < 16:
                    ev_evacR = ev_s
                ev_evacH = ev_s
            ev_R_T_done = None

            xin_ev = [None] * 16
            for m in range(32):
                r, evw = ring_load(winb_s[m, :, :])
                if k == 0:
                    bb.W(sy, ev_prep_all)
                pb = next_aux()
                bb.W(te, aux_free[pb])
                bb.W(te, evw)
                bb.W(te, ev_evacH)
                for kk in range(8):
                    ins = te.matmul(PB[pb][:, 0:TC], ring[r][:, kk * 128:(kk + 1) * 128],
                                    h0T[kk][:, :], start=(kk == 0), stop=(kk == 7))
                ev_t = T.inc(ins)
                ring_free[r] = ev_t
                bb.W(se, ev_t)
                bb.W(se, scan_done_prev)
                bb.W(se, wout_T_prev)
                if m < 16:
                    ins = se.activation(xin[m][:, 3:3 + TC], PB[pb][:, 0:TC], AF.Copy)
                else:
                    ins = se.activation(zsil[m - 16][:, :], PB[pb][:, 0:TC], AF.Silu)
                ev_s = S.inc(ins)
                aux_free[pb] = ev_s
                if m < 16:
                    xin_ev[m] = ev_s
            ev_xin = (S, S.n)
            fpump(k - 1, frac=0.25, budget=2)
            if k == 0:
                dbg_dump("sc0", scol[:, 0:16], ev_ln)
                dbg_dump("yb0", ybuf[0][:, :], ev_ln)
                dbg_dump("h0T0", h0T[0][:, :], ev_evacH)
                dbg_dump("RT0", RT[0][:, :], ev_evacR)
                dbg_dump("zsil0", zsil[0][:, :], ev_xin)
                dbg_dump("A0", A_sb[:, :], ev_vA)

            ev_xc = None
            silu_ev = [None, None]
            for i in range(16):
                ca = cacc[i % 2]
                bb.W(ve, xin_ev[i])
                bb.W(ve, silu_ev[i % 2])
                ve.tensor_scalar(ca[:, :], xin[i][:, 0:TC], cw_sb[:, 4 * i:4 * i + 1],
                                 None, op0=OP.mult)
                for j in range(1, 4):
                    ins = ve.scalar_tensor_tensor(ca[:, :], in0=xin[i][:, j:j + TC],
                                                  scalar=cw_sb[:, 4 * i + j:4 * i + j + 1],
                                                  in1=ca[:, :], op0=OP.mult, op1=OP.add)
                ev_cv = V.inc(ins)
                bb.W(ge, ev_cv)
                ins = ge.tensor_copy(xin[i][:, 0:3], xin[i][:, TC:TC + 3])
                ev_gt = G.inc(ins)
                bb.W(se, ev_cv)
                bb.W(se, ev_gt)
                ins = se.activation(xin[i][:, 3:3 + TC], ca[:, :], AF.Silu,
                                    bias=cb_sb[:, i:i + 1])
                ev_xc = S.inc(ins)
                silu_ev[i % 2] = ev_xc
            ev_xc_all = ev_xc
            fpump(k - 1, frac=0.35, budget=2)
            if k == 0:
                dbg_dump("xin0", xin[0][:, 0:TC + 3], ev_xc_all)
                dbg_dump("xpb", xprojb[:, :], ev_cast)
                if dbg:
                    for i2 in range(16):
                        bb.W(sy, ev_xc_all)
                        ins = sy.dma_start(out=dbg_outs["xcall"][:, i2 * TC:(i2 + 1) * TC],
                                           in_=xin[i2][:, 3:3 + TC])
                        DS("dbg").inc(ins, 16)

            pb_dbc = next_aux()
            bb.W(te, aux_free[pb_dbc])
            bb.W(te, ev_xc_all)
            bb.W(te, ev_cast)
            for kk in range(16):
                ins = te.matmul(PB[pb_dbc][0:96, 0:TC], xprojb[:, kk * 96:(kk + 1) * 96],
                                xin[kk][:, 3:3 + TC], start=(kk == 0), stop=(kk == 15))
            ev_dbc = T.inc(ins)
            bb.W(se, ev_dbc)
            ins = se.activation(dt_b[:, :], PB[pb_dbc][0:RK, 0:TC], AF.Copy)
            ev_dt = S.inc(ins)
            bb.W(ve, ev_dbc)
            bb.W(ve, ev_dt)
            bb.W(ve, BC_free)
            ins = ve.tensor_copy(BC_sb[64:96, :], PB[pb_dbc][64:96, 0:TC])
            ev_bc_ev = V.inc(ins)
            aux_free[pb_dbc] = ev_bc_ev
            bcn_ev = [None] * 16
            ccn_ev = [None] * 16
            for cc in range(2):
                ev_bcrow = dma(sy, DS("bcrow"), BCrow[0:1, :],
                               BC_sb[64 + cc * 16:80 + cc * 16, :],
                               wait=[ev_bc_ev, bcrow_free[0]])
                for n in range(16):
                    pb = next_aux()
                    bb.W(te, aux_free[pb])
                    bb.W(te, ev_bcrow)
                    ins = te.matmul(PB[pb][:, 0:TC], ones_bf[0:1, :],
                                    BCrow[0:1, n * TC:(n + 1) * TC], start=True, stop=True)
                    ev_t = T.inc(ins)
                    dst = B_bc[n] if cc == 0 else C_bc[n]
                    bb.W(se, ev_t)
                    bb.W(se, scan_done_prev)
                    ins = se.activation(dst[:, :], PB[pb][:, 0:TC], AF.Copy)
                    ev_s = S.inc(ins)
                    aux_free[pb] = ev_s
                    (bcn_ev if cc == 0 else ccn_ev)[n] = ev_s
                bcrow_free[0] = (T, T.n)
            BC_done = (S, S.n)
            BC_free = ev_bcrow
            fpump(k - 1, frac=0.45, budget=2)
            if k == 0:
                dbg_dump("Bbc0", B_bc[0][:, :], BC_done)
                dbg_dump("Cbc0", C_bc[0][:, :], BC_done)
                dbg_dump("dtb0", dt_b[:, :], ev_dt)

            dl_s = [None, None]
            ev_dl_p = [None, None]

            def delta_chain(i):
                pb = next_aux()
                bb.W(te, aux_free[pb])
                bb.W(te, ev_dt)
                bb.W(te, ev_cast)
                ins = te.matmul(PB[pb][:, 0:TC], wdtb[:, i * 128:(i + 1) * 128],
                                dt_b[:, :], start=True, stop=True)
                ev_t = T.inc(ins)
                sp = sp_t[i % 2]
                bb.W(se, ev_t)
                bb.W(se, spv_ev[i % 2])
                se.activation(sp[:, :], PB[pb][:, 0:TC], AF.Abs, bias=bdt_sb[:, i:i + 1])
                se.activation(sp[:, :], sp[:, :], AF.Exp, scale=-1.0)
                ins = se.activation(sp[:, :], sp[:, :], AF.Ln, bias=1.0)
                ev_sp = S.inc(ins)
                dl = dlt[i % 2]
                bb.W(ve, ev_sp)
                bb.W(ve, dls_ev[i % 2])
                ve.scalar_tensor_tensor(dl[:, :], in0=PB[pb][:, 0:TC],
                                        scalar=bdt_sb[:, i:i + 1], in1=sp[:, :],
                                        op0=OP.add, op1=OP.add)
                ins = ve.tensor_tensor(dl[:, :], dl[:, :], sp[:, :], op=OP.max)
                ev_dl = V.inc(ins)
                aux_free[pb] = ev_dl
                spv_ev[i % 2] = ev_dl
                d_u = du[i % 2]
                ins = ve.tensor_tensor(d_u[:, :], dl[:, :], xin[i][:, 3:3 + TC], op=OP.mult)
                V.inc(ins)
                ev_dl_p[i % 2] = ev_dl
                if k == 0 and i == 0:
                    dbg_dump("dlt0", dl[:, :], ev_dl)

            delta_chain(0)
            for i in range(16):
                dl = dlt[i % 2]
                d_u = du[i % 2]
                bb.W(se, ev_dl_p[i % 2])
                ev_scan = None
                scan_ev11 = None
                for n in range(NST):
                    sl = a_t[n % 4]
                    bb.W(se, a_slot_ev[n % 4])
                    ins = se.activation(sl[:, :], dl[:, :], AF.Exp,
                                        scale=A_sb[:, i * 16 + n: i * 16 + n + 1])
                    a_ev = S.inc(ins)
                    bbuf = b_t[n % 2]
                    ss = s_all[:, n * TC:(n + 1) * TC]
                    bb.W(ve, bcn_ev[n])
                    ve.tensor_tensor(bbuf[:, :], d_u[:, :], B_bc[n][:, :], op=OP.mult)
                    bb.W(ve, a_ev)
                    ins = ve.tensor_tensor_scan(
                        ss, sl[:, :], bbuf[:, :],
                        initial=s_state[:, i * 16 + n: i * 16 + n + 1],
                        op0=OP.mult, op1=OP.add)
                    ev_scan = V.inc(ins)
                    a_slot_ev[n % 4] = ev_scan
                dls_ev[i % 2] = (S, S.n)
                if i + 1 < 16:
                    delta_chain(i + 1)
                bb.W(ve, ev_scan)
                ve.tensor_copy(s_state[:, i * 16:(i + 1) * 16],
                               s_all[:, TC - 1::TC])
                bb.W(ve, ccn_ev[15])
                for n in range(NST):
                    ve.tensor_tensor(s_all[:, n * TC:(n + 1) * TC],
                                     s_all[:, n * TC:(n + 1) * TC],
                                     C_bc[n][:, :], op=OP.mult)
                for step in (1, 2, 4, 8):
                    for base in range(0, NST, 2 * step):
                        ve.tensor_tensor(s_all[:, base * TC:(base + 1) * TC],
                                         s_all[:, base * TC:(base + 1) * TC],
                                         s_all[:, (base + step) * TC:(base + step + 1) * TC],
                                         op=OP.add)
                ve.scalar_tensor_tensor(yacc[:, :], in0=xin[i][:, 3:3 + TC],
                                        scalar=dp_sb[:, i:i + 1],
                                        in1=s_all[:, 0:TC],
                                        op0=OP.mult, op1=OP.add)
                ins = ve.tensor_tensor(xin[i][:, 3:3 + TC], yacc[:, :],
                                       zsil[i][:, :], op=OP.mult)
                ev_yg = V.inc(ins)
                if k == 0 and i == 0:
                    dbg_dump("yg0", xin[i][:, 3:3 + TC], ev_yg)
                fpump(k - 1, budget=1, pace=False)
            scan_done_prev = (V, V.n)
            ev_yg_all = scan_done_prev

            bb.W(ve, ev_evacR)
            for dh in range(2):
                for i in range(16):
                    r, evw = ring_load(woutb_s[i, :, :])
                    bb.W(te, evw)
                    bb.W(te, ev_yg_all)
                    if i == 0:
                        bb.W(te, psout_free)
                    for m in range(4):
                        mm = dh * 4 + m
                        pbo = 4 + m // 2
                        ins = te.matmul(PB[pbo][:, (m % 2) * TC:(m % 2 + 1) * TC],
                                        ring[r][:, mm * 128:(mm + 1) * 128],
                                        xin[i][:, 3:3 + TC],
                                        start=(i == 0 and m % 2 == 0),
                                        stop=(i == 15 and m % 2 == 1))
                    ev_wout_last = T.inc(ins)
                    ring_free[r] = ev_wout_last
                bb.W(ve, ev_wout_last)
                for m in range(4):
                    mm = dh * 4 + m
                    pbo = 4 + m // 2
                    ins = ve.tensor_tensor(RT[mm][:, :], RT[mm][:, :],
                                           PB[pbo][:, (m % 2) * TC:(m % 2 + 1) * TC],
                                           op=OP.add)
                psout_free = V.inc(ins)
            ev_x1T = psout_free
            wout_T_prev = (T, T.n)
            fpump(k - 1, frac=0.97, budget=2)

            ev_x1tok = None
            for g0 in range(0, 16, 4):
                pb = next_tr()
                bb.W(te, tr_free[pb])
                bb.W(te, ev_x1T)
                for jj in range(4):
                    idx = g0 + jj
                    tt, dc = idx // 8, idx % 8
                    ins = te.transpose(PB[pb][:, jj * 128:(jj + 1) * 128],
                                       RT[dc][:, tt * 128:(tt + 1) * 128], idf[:, :])
                ev_t = T.inc(ins)
                bb.W(se, ev_t)
                for jj in range(4):
                    idx = g0 + jj
                    tt, dc = idx // 8, idx % 8
                    ins = se.activation(xbuf[tt][:, dc * 128:(dc + 1) * 128],
                                        PB[pb][:, jj * 128:(jj + 1) * 128], AF.Copy)
                ev_s = S.inc(ins)
                tr_free[pb] = ev_s
                ev_x1tok = ev_s
            RT_evac_wait = (T, T.n)

            ev_xn1 = None
            for tt in range(2):
                c = lambda j: scol[:, par * 16 + tt * 8 + j: par * 16 + tt * 8 + j + 1]
                bb.W(ve, ev_x1tok)
                ev_red = V.inc(ve.reduce_sum(c(0), xbuf[tt][:, :], axis=AX.X))
                bb.W(se, ev_x1tok)
                se.activation(junk[:, :], xbuf[tt][:, :], AF.Square, accum_out=c(1))
                ins = se.activation(c(1), c(1), AF.Copy)
                ev_sq = S.inc(ins)
                bb.W(ve, ev_red)
                vf(ve.tensor_scalar(c(2), c(0), 1.0 / D, None, op0=OP.mult))
                bb.W(ve, ev_sq)
                vf(ve.tensor_scalar(c(3), c(1), 1.0 / D, None, op0=OP.mult))
                vf(ve.tensor_tensor(c(4), c(2), c(2), op=OP.mult))
                ins = ve.tensor_tensor(c(3), c(3), c(4), op=OP.subtract)
                ev_var = V.inc(ins)
                bb.W(se, ev_var)
                sf(se.activation(c(4), c(3), AF.Ln, bias=EPS))
                ins = se.activation(c(4), c(4), AF.Exp, scale=-0.5)
                ev_r3 = S.inc(ins)
                bb.W(ve, ev_r3)
                bb.W(ve, ybuf_free[tt])
                ins = ve.tensor_scalar(ybuf[tt][:, :], xbuf[tt][:, :], c(2), c(4),
                                        op0=OP.subtract, op1=OP.mult)
                ev_xn1 = V.inc(ins)
                ev_st1 = dma(sy, DS(f"xb{tt}"), x1_s[t0 + tt * 128: t0 + (tt + 1) * 128, :],
                             xbuf[tt][:, :], wait=ev_x1tok)
                ev_st2 = dma(sy, DS(f"yb{tt}"), xn1_s[t0 + tt * 128: t0 + (tt + 1) * 128, :],
                             ybuf[tt][:, :], wait=ev_xn1)
                xbuf_free[tt] = [ev_st1]
            ev_xn1_all = ev_xn1

            ev_xn1T = None
            for g0 in range(0, 16, 4):
                pb = next_tr()
                bb.W(te, tr_free[pb])
                bb.W(te, ev_xn1_all)
                for jj in range(4):
                    idx = g0 + jj
                    tt, dc = idx // 8, idx % 8
                    ins = te.transpose(PB[pb][:, jj * 128:(jj + 1) * 128],
                                       ybuf[tt][:, dc * 128:(dc + 1) * 128], idf[:, :])
                ev_t = T.inc(ins)
                bb.W(se, ev_t)
                bb.W(se, xn1T_store_ev)
                for jj in range(4):
                    idx = g0 + jj
                    tt, dc = idx // 8, idx % 8
                    ins = se.activation(xn1T[dc][:, tt * 128:(tt + 1) * 128],
                                        PB[pb][:, jj * 128:(jj + 1) * 128], AF.Copy)
                ev_s = S.inc(ins)
                tr_free[pb] = ev_s
                ev_xn1T = ev_s
            for tt in range(2):
                ybuf_free[tt] = [(T, T.n), (dsm[f"yb{tt}"], dsm[f"yb{tt}"].n)]
            ev_xn1T_st = None
            for dc in range(8):
                ev_xn1T_st = dma(sy, DS("xt"), xn1T_s[dc * 128:(dc + 1) * 128, t0:t0 + TC],
                                 xn1T[dc][:, :], wait=ev_xn1T)
            xn1T_store_ev = ev_xn1T_st

            pb = next_aux()
            bb.W(te, aux_free[pb])
            bb.W(te, ev_xn1T)
            for kk in range(8):
                ins = te.matmul(PB[pb][0:NE, 0:TC], wgb[:, kk * 8:(kk + 1) * 8],
                                xn1T[kk][:, :], start=(kk == 0), stop=(kk == 7))
            ev_t = T.inc(ins)
            bb.W(se, ev_t)
            ins = se.activation(lg_sb[:, :], PB[pb][0:NE, 0:TC], AF.Identity,
                                bias=bg_c[:, :])
            ev_lg = S.inc(ins)
            aux_free[pb] = ev_lg
            pbt = next_aux()
            bb.W(te, aux_free[pbt])
            bb.W(te, ev_lg)
            for tt in range(2):
                ins = te.transpose(PB[pbt][0:128, tt * 8:(tt + 1) * 8],
                                   lg_sb[:, tt * 128:(tt + 1) * 128], idf[0:NE, 0:NE])
            ev_t = T.inc(ins)
            bb.W(ve, ev_t)
            for tt in range(2):
                vf(ve.tensor_copy(lgt[tt][:, :], PB[pbt][0:128, tt * 8:(tt + 1) * 8]))
            ev_lgt = (V, V.n)
            aux_free[pbt] = ev_lgt
            ev_gts = None
            bb.W(ve, gates_T_ev)
            bb.W(ve, gt_store_prev)
            for tt in range(2):
                c = lambda j: gcol[:, par * 16 + tt * 8 + j: par * 16 + tt * 8 + j + 1]
                ev_rm1 = V.inc(ve.reduce_max(c(0), lgt[tt][:, :], axis=AX.X))
                bb.W(ve, ev_rm1)
                vf(ve.tensor_scalar(eq1[tt][:, :], lgt[tt][:, :], c(0), None, op0=OP.is_ge))
                vf(ve.scalar_tensor_tensor(msk[tt][:, :], in0=eq1[tt][:, :], scalar=-1e30,
                                           in1=lgt[tt][:, :], op0=OP.mult, op1=OP.add))
                ev_rm2 = V.inc(ve.reduce_max(c(1), msk[tt][:, :], axis=AX.X))
                bb.W(ve, ev_rm2)
                ins = ve.tensor_tensor(c(2), c(1), c(0), op=OP.subtract)
                ev_d12 = V.inc(ins)
                bb.W(se, ev_d12)
                ins = se.activation(c(3), c(2), AF.Exp)
                ev_e = S.inc(ins)
                bb.W(ve, ev_e)
                vf(ve.tensor_scalar(c(4), c(3), 1.0, None, op0=OP.add))
                vf(ve.reciprocal(c(4), c(4)))
                vf(ve.tensor_scalar(c(5), c(4), -1.0, 1.0, op0=OP.mult, op1=OP.add))
                vf(ve.tensor_scalar(msk[tt][:, :], msk[tt][:, :], c(1), None, op0=OP.is_ge))
                vf(ve.tensor_scalar(gts[tt][:, :], eq1[tt][:, :], c(4), None, op0=OP.mult))
                ins = ve.scalar_tensor_tensor(gts[tt][:, :], in0=msk[tt][:, :], scalar=c(5),
                                              in1=gts[tt][:, :], op0=OP.mult, op1=OP.add)
                ev_gts = V.inc(ins)
            pbt = next_aux()
            bb.W(te, aux_free[pbt])
            bb.W(te, ev_gts)
            for tt in range(2):
                ins = te.transpose(PB[pbt][0:NE, tt * 128:(tt + 1) * 128],
                                   gts[tt][:, :], idf[:, :])
            ev_t = T.inc(ins)
            gates_T_ev = ev_t
            bb.W(se, ev_t)
            bb.W(se, g_store_ev)
            for tt in range(2):
                ins = se.activation(g_sb[:, tt * 128:(tt + 1) * 128],
                                    PB[pbt][0:NE, tt * 128:(tt + 1) * 128], AF.Copy)
            ev_gsb = S.inc(ins)
            aux_free[pbt] = ev_gsb
            g_store_ev = dma(sy, DS("g"), g_s[:, t0:t0 + TC], g_sb[:, :], wait=ev_gsb)
            ev_gt_st = None
            for tt in range(2):
                ev_gt_st = dma(sy, DS("gt"), gt_s[t0 + tt * 128: t0 + (tt + 1) * 128, :],
                               gts[tt][:, :], wait=ev_gts if tt == 1 else None)
            gt_store_prev = ev_gt_st
            mstore_ev[k] += [xbuf_free[0][0], xbuf_free[1][0],
                             (dsm["yb0"], dsm["yb0"].n), (dsm["yb1"], dsm["yb1"].n),
                             (dsm["xt"], dsm["xt"].n), g_store_ev, ev_gt_st]
            fpump(k, budget=2)


        while fs["j"] < NFC:
            fpump(NCH - 1, budget=100, pace=False)

        bb.W(sy, fstate["last_out_ev"])
        bb.W(sy, (dsm["out"], dsm["out"].n))

    return nc


_NC_CACHE = {}


def _get_nc():
    if "nc" not in _NC_CACHE:
        _NC_CACHE["nc"] = build_nc()
    return _NC_CACHE["nc"]


def kernel(**inputs):
    g = lambda n: np.asarray(inputs[n])
    x = g("L0_attn_x")
    f32 = np.float32

    common = {
        "w_in": np.ascontiguousarray(g("L0_attn_W_in"), dtype=f32),
        "w_out": np.ascontiguousarray(g("L0_attn_W_out"), dtype=f32),
        "w_xp": np.ascontiguousarray(g("L0_attn_W_xproj"), dtype=f32),
        "w_dt": np.ascontiguousarray(g("L0_attn_W_dt"), dtype=f32),
        "w1": np.ascontiguousarray(g("L0_ffn_W1"), dtype=f32),
        "w2": np.ascontiguousarray(g("L0_ffn_W2"), dtype=f32),
        "wg": np.ascontiguousarray(g("L0_ffn_Wg"), dtype=f32),
        "lng": np.ascontiguousarray(g("L0_attn_ln_g").reshape(1, D), dtype=f32),
        "lnb": np.ascontiguousarray(g("L0_attn_ln_b").reshape(1, D), dtype=f32),
        "gam": np.ascontiguousarray(g("L0_ffn_gamma").reshape(1, D), dtype=f32),
        "bet": np.ascontiguousarray(g("L0_ffn_beta").reshape(1, D), dtype=f32),
        "alog": np.ascontiguousarray(
            g("L0_attn_A_log").reshape(16, 128, NST).transpose(1, 0, 2).reshape(128, 16 * NST),
            dtype=f32),
        "cw": np.ascontiguousarray(
            g("L0_attn_conv_w").reshape(DI, 4).reshape(16, 128, 4)
            .transpose(1, 0, 2).reshape(128, 64), dtype=f32),
        "cb": np.ascontiguousarray(g("L0_attn_conv_b").reshape(16, 128).T, dtype=f32),
        "bdt": np.ascontiguousarray(g("L0_attn_b_dt").reshape(16, 128).T, dtype=f32),
        "dpar": np.ascontiguousarray(g("L0_attn_D_param").reshape(16, 128).T, dtype=f32),
        "b1": np.ascontiguousarray(
            g("L0_ffn_b1").reshape(NE, 16, 128).transpose(2, 0, 1).reshape(128, 128),
            dtype=f32),
        "b2": np.ascontiguousarray(g("L0_ffn_b2"), dtype=f32),
        "bg": np.ascontiguousarray(g("L0_ffn_bg").reshape(NE, 1), dtype=f32),
    }
    in_maps = [dict(common, x=np.ascontiguousarray(x[b], dtype=f32)) for b in range(8)]
    nc = _get_nc()
    res = run_bass_kernel_spmd(nc, in_maps, core_ids=list(range(8)))
    out = np.stack([np.asarray(r["out"]) for r in res.results], axis=0)
    return out.astype(np.float32)


if __name__ == "__main__":
    nc = build_nc()
    print("built ok")
